# version 03
# baseline (speedup 1.0000x reference)
import numpy as np
from contextlib import ExitStack
import concourse.bass as bass
import concourse.mybir as mybir
from concourse.bass_utils import run_bass_kernel_spmd

F32 = mybir.dt.float32
BF16 = mybir.dt.bfloat16
AF = mybir.ActivationFunctionType
ALU = mybir.AluOpType
AXX = mybir.AxisListType.X

D = 2048
DC = 16
HID = 8192
HC = 64
NT = 2048
T = 4096
TILE = 512
NTILE = 4
HQ = 16
HKV = 4
KB = 512
SCALE = float(1.0 / np.sqrt(128.0))
BSHIFT = -40.0
MASKVAL = -1e30
EPS = 1e-5
RMAGIC = 12582912.0
TWO_PI_HI = 6.28125
TWO_PI_LO = 0.0019353071795864769
LN_THETA = float(np.log(10000.0))
INV2PI = float(1.0 / (2 * np.pi))

SLOTS = [[], []]
for _i in range(8):
    SLOTS[0] += [2 * _i, 31 - 2 * _i]
    SLOTS[1] += [2 * _i + 1, 30 - 2 * _i]
EXT = [g // 4 + 1 for g in SLOTS[0]]
assert EXT == [g // 4 + 1 for g in SLOTS[1]]

ENGS = ("sync", "vector", "scalar", "tensor", "gpsimd")
SWAP32 = [i ^ 1 for i in range(32)]


class Builder:

    def __init__(self):
        self.st = {e: [] for e in ENGS}
        self.cnt = {}

    def op(self, eng, meth, *a, inc=None, incn=1, waits=(), **kw):
        self.st[eng].append((tuple(waits), meth, a, kw, inc, incn))
        if inc is not None:
            nm = inc.name
            self.cnt[nm] = self.cnt.get(nm, 0) + incn
            return self.cnt[nm]
        return None

    def c(self, sem):
        return self.cnt.get(sem.name, 0)

    def realize_engine(self, engobj, eng):
        seen = {}
        for waits, meth, a, kw, inc, incn in self.st[eng]:
            for semh, val in waits:
                if val <= 0 or seen.get(semh.name, -1) >= val:
                    continue
                seen[semh.name] = val
                engobj.wait_ge(semh, val)
            if meth is None:
                continue
            ins = getattr(engobj, meth)(*a, **kw)
            if inc is not None:
                ins.then_inc(inc, incn)


class NS:
    pass


def build_nc(debug=False, cc_groups=None, limit=5, alimit=4):
    nc = bass.Bass("TRN2", target_bir_lowering=False,
                   detect_race_conditions=False)

    x_d = nc.dram_tensor("x", [NT, D], F32, kind="ExternalInput")
    qw_d = nc.dram_tensor("qw", [D, D], F32, kind="ExternalInput")
    kvw_d = nc.dram_tensor("kvw", [1024, D], F32, kind="ExternalInput")
    cw_d = nc.dram_tensor("cw", [D, D], F32, kind="ExternalInput")
    wg_d = nc.dram_tensor("wg", [HID, D], F32, kind="ExternalInput")
    wu_d = nc.dram_tensor("wu", [HID, D], F32, kind="ExternalInput")
    wd_d = nc.dram_tensor("wd", [D, HID], F32, kind="ExternalInput")
    qpos_d = nc.dram_tensor("qpos", [1, 16], F32, kind="ExternalInput")
    out_d = nc.dram_tensor("out", [NT, D], F32, kind="ExternalOutput")

    wg_t = nc.dram_tensor("wg_t", [HID, D], BF16)
    wu_t = nc.dram_tensor("wu_t", [HID, D], BF16)
    wd_t = nc.dram_tensor("wd_t", [D, HID], BF16)
    qw_t = nc.dram_tensor("qw_t", [D, D], BF16)
    kvw_t = nc.dram_tensor("kvw_t", [1024, D], BF16)
    cw_t = nc.dram_tensor("cw_t", [D, D], BF16)
    x1_d = nc.dram_tensor("x1", [D, NT], F32)
    x2_d = nc.dram_tensor("x2", [D, NT], F32)
    q_d = nc.dram_tensor("qbuf", [D, NT], BF16)
    kvag_in = nc.dram_tensor("kvag_in", [1024, NT], BF16)
    kvag_out = nc.dram_tensor("kvag_out", [2048, NT], BF16)
    if debug:
        dbg = {nm: nc.dram_tensor("dbg_" + nm, [D, NT], F32, kind="ExternalOutput")
               for nm in ("x1", "x2")}
        dbg["q"] = nc.dram_tensor("dbg_q", [D, NT], BF16, kind="ExternalOutput")
        dbg["x3"] = nc.dram_tensor("dbg_x3", [D, NT], F32, kind="ExternalOutput")
        dbg["xn5"] = nc.dram_tensor("dbg_xn5", [128, DC * TILE], BF16, kind="ExternalOutput")
        dbg["ropec"] = nc.dram_tensor("dbg_ropec", [128, NTILE * TILE], F32, kind="ExternalOutput")
        dbg["consts"] = nc.dram_tensor("dbg_consts", [128, 10 * 168], F32, kind="ExternalOutput")
        dbg["ropesp"] = nc.dram_tensor("dbg_ropesp", [128, NTILE * TILE], F32, kind="ExternalOutput")
        dbg["kv"] = nc.dram_tensor("dbg_kv", [2048, NT], BF16, kind="ExternalOutput")

    b = Builder()
    es = ExitStack()
    with es:
        sbuf = lambda name, shape, dt: es.enter_context(nc.sbuf_tensor(name, shape, dt))
        sem = lambda name: es.enter_context(nc.semaphore(name))

        s_pe = sem("s_pe")
        s_dve = sem("s_dve")
        s_act = sem("s_act")
        s_pool = sem("s_pool")
        s_w0 = sem("s_w0")
        s_w1 = sem("s_w1")
        s_v0 = sem("s_v0")
        s_v1 = sem("s_v1")
        s_ldt = sem("s_ldt")
        STS = [sem(f"s_st{i}") for i in range(6)]
        s_cc = sem("s_cc")
        WPAR = (s_w0, s_w1)
        VPAR = (s_v0, s_v1)

        def adw():
            return [(s_pe, b.c(s_pe)), (s_dve, b.c(s_dve)),
                    (s_act, b.c(s_act)), (s_pool, b.c(s_pool))]

        def st_all():
            return [(s, b.c(s)) for s in STS]

        ident_f = sbuf("ident_f", [128, 128], F32)
        ident_b = sbuf("ident_b", [128, 128], BF16)
        jtile = sbuf("jtile", [128, 128], F32)
        dtile = sbuf("dtile", [128, KB], F32)
        iota_p = sbuf("iota_p", [128, 1], F32)
        halfp = sbuf("halfp", [128, 1], F32)
        freq = sbuf("freq", [128, 1], F32)
        altsign = sbuf("altsign", [128, 1], F32)
        negB = sbuf("negB", [128, 1], F32)
        ones_row = sbuf("ones_row", [1, 128], F32)
        qpos_sb = sbuf("qpos_sb", [1, 16], F32)
        qpos_cols = sbuf("qpos_cols", [128, 16], F32)
        qf_all = sbuf("qf_all", [128, 16], F32)
        rs4 = sbuf("rs4", [128, 4], F32)
        ssq4 = sbuf("ssq4", [128, 4], F32)
        mean4 = sbuf("mean4", [128, 4], F32)
        esq4 = sbuf("esq4", [128, 4], F32)
        m24 = sbuf("m24", [128, 4], F32)
        var4 = sbuf("var4", [128, 4], F32)
        rvar4 = sbuf("rvar4", [128, 4], F32)
        rstd4 = sbuf("rstd4", [128, 4], F32)
        sums4 = sbuf("sums4", [128, 4], F32)
        recip4 = sbuf("recip4", [128, 4], F32)

        block = es.enter_context(nc.Block())

        def dump_consts(half):
            if not debug:
                return
            col = half * 168
            srcs = [(iota_p, 1), (halfp, 1), (freq, 1), (altsign, 1), (negB, 1),
                    (qf_all, 16), (qpos_cols, 16), (jtile, 128), (dtile, 3)]
            for tsr, wdt in srcs:
                b.op("sync", "dma_start",
                     out=dbg["consts"].ap()[:, col:col + wdt],
                     in_=tsr[:, 0:wdt], waits=adw(), inc=STS[5], incn=16,
                     allow_slow_non_contiguous=True)
                col += wdt

        b.op("gpsimd", "iota", ident_f[:], pattern=[[0, 128]], base=0,
             channel_multiplier=1, allow_small_or_imprecise_dtypes=True)
        b.op("gpsimd", "iota", jtile[:], pattern=[[1, 128]], base=0,
             channel_multiplier=0, allow_small_or_imprecise_dtypes=True)
        b.op("gpsimd", "iota", dtile[:], pattern=[[1, KB]], base=0,
             channel_multiplier=-1, allow_small_or_imprecise_dtypes=True)
        b.op("gpsimd", "iota", iota_p[:], pattern=[[0, 1]], base=0,
             channel_multiplier=1, allow_small_or_imprecise_dtypes=True,
             inc=s_pool)
        g0 = b.c(s_pool)
        b.op("vector", "tensor_tensor", ident_f[:], ident_f[:], jtile[:],
             ALU.is_equal, waits=[(s_pool, g0)], inc=s_dve)
        b.op("vector", "tensor_copy", out=ident_b[:], in_=ident_f[:],
             waits=[(s_dve, b.c(s_dve))], inc=s_dve)
        b.op("vector", "tensor_scalar", out=halfp[:], in0=iota_p[:],
             scalar1=0.5, scalar2=0.25, op0=ALU.mult, op1=ALU.subtract,
             inc=s_dve)
        b.op("vector", "tensor_scalar", out=halfp[:], in0=halfp[:],
             scalar1=RMAGIC, scalar2=RMAGIC, op0=ALU.add, op1=ALU.subtract,
             waits=[(s_dve, b.c(s_dve))], inc=s_dve)
        b.op("vector", "tensor_scalar_mul", freq[:], halfp[:],
             -2.0 * LN_THETA / 128.0, waits=[(s_dve, b.c(s_dve))], inc=s_dve)
        fq = b.c(s_dve)
        b.op("scalar", "activation", out=freq[:], in_=freq[:], func=AF.Exp,
             waits=[(s_dve, fq)], inc=s_act)
        fq_a = b.c(s_act)
        b.op("vector", "tensor_scalar_mul", altsign[:], halfp[:], -2.0,
             inc=s_dve)
        b.op("vector", "tensor_add", altsign[:], altsign[:], iota_p[:],
             waits=[(s_dve, b.c(s_dve))], inc=s_dve)
        b.op("vector", "tensor_scalar", out=altsign[:], in0=altsign[:],
             scalar1=2.0, scalar2=-1.0, op0=ALU.mult, op1=ALU.add,
             waits=[(s_dve, b.c(s_dve))], inc=s_dve)
        b.op("vector", "memset", negB[:], BSHIFT)
        b.op("vector", "memset", ones_row[:], 1.0, inc=s_dve)
        consts_v = b.c(s_dve)

        b.op("sync", "dma_start", out=qpos_sb[:], in_=qpos_d.ap(),
             inc=s_ldt, incn=16)
        qpos_ld = b.c(s_ldt)

        prep_list = [
            (wg_d, wg_t, HC, DC),
            (wu_d, wu_t, HC, DC),
            (wd_d, wd_t, DC, HC),
            (qw_d, qw_t, DC, DC),
            (kvw_d, kvw_t, 8, DC),
            (cw_d, cw_t, DC, DC),
        ]
        with ExitStack() as pes:
            ps_qposb = pes.enter_context(nc.psum_tensor("ps_qposb", [128, 16], F32))
            ps_prep = [pes.enter_context(nc.psum_tensor(f"ps_prep{i}", [128, 128], BF16))
                       for i in range(4)]
            slab_f = pes.enter_context(nc.sbuf_tensor("slab_f", [128, 2 * HID], F32))
            slab_b = pes.enter_context(nc.sbuf_tensor("slab_b", [128, 2 * HID], BF16))
            tout = pes.enter_context(nc.sbuf_tensor("tout", [128, 6 * 128], BF16))

            b.op("tensor", "matmul", ps_qposb[:], ones_row[:], qpos_sb[:],
                 start=True, stop=True,
                 waits=[(s_ldt, qpos_ld), (s_dve, consts_v)], inc=s_pe)
            b.op("vector", "tensor_copy", out=qpos_cols[:], in_=ps_qposb[:],
                 waits=[(s_pe, b.c(s_pe))], inc=s_dve)
            b.op("vector", "tensor_scalar", out=qf_all[:], in0=qpos_cols[:],
                 scalar1=freq[:], scalar2=None, op0=ALU.mult,
                 waits=[(s_act, fq_a), (s_dve, b.c(s_dve))], inc=s_dve)

            slab_i = 0
            tcount = 0
            conv_marks, tr_marks, evac_marks, st_marks = {}, {}, {}, {}
            for src, dst, nrc, ncc in prep_list:
                cols = ncc * 128
                for rc in range(nrc):
                    par = slab_i % 2
                    w = []
                    if slab_i >= 2:
                        w.append((s_dve, conv_marks[slab_i - 2]))
                    b.op("sync", "dma_start",
                         out=slab_f[:, par * HID: par * HID + cols],
                         in_=src.ap()[rc * 128:(rc + 1) * 128, :],
                         waits=w, inc=WPAR[par], incn=16)
                    ldv = b.c(WPAR[par])
                    w = [(WPAR[par], ldv)]
                    if slab_i >= 2:
                        w.append((s_pe, tr_marks[slab_i - 2]))
                    b.op("vector", "tensor_copy",
                         out=slab_b[:, par * HID: par * HID + cols],
                         in_=slab_f[:, par * HID: par * HID + cols],
                         waits=w, inc=s_dve)
                    conv_marks[slab_i] = b.c(s_dve)
                    for cc in range(ncc):
                        pp = tcount % 4
                        tb = tcount % 6
                        w = [(s_dve, conv_marks[slab_i])]
                        if tcount >= 4:
                            w.append(evac_marks[tcount - 4])
                        b.op("tensor", "transpose", ps_prep[pp][:],
                             slab_b[:, par * HID + cc * 128: par * HID + (cc + 1) * 128],
                             ident_b[:], waits=w, inc=s_pe)
                        pev = b.c(s_pe)
                        w = [(s_pe, pev)]
                        if tcount >= 6:
                            w.append(st_marks[tcount - 6])
                        if tcount % 2 == 0:
                            b.op("vector", "tensor_copy",
                                 out=tout[:, tb * 128:(tb + 1) * 128],
                                 in_=ps_prep[pp][:], waits=w, inc=s_dve)
                            evac_marks[tcount] = (s_dve, b.c(s_dve))
                        else:
                            b.op("scalar", "copy",
                                 out=tout[:, tb * 128:(tb + 1) * 128],
                                 in_=ps_prep[pp][:], waits=w, inc=s_act)
                            evac_marks[tcount] = (s_act, b.c(s_act))
                        stsem = STS[tcount % 2]
                        b.op("sync", "dma_start",
                             out=dst.ap()[rc * 128:(rc + 1) * 128,
                                          cc * 128:(cc + 1) * 128],
                             in_=tout[:, tb * 128:(tb + 1) * 128],
                             waits=[evac_marks[tcount]], inc=stsem, incn=16)
                        st_marks[tcount] = (stsem, b.c(stsem))
                        tcount += 1
                    tr_marks[slab_i] = b.c(s_pe)
                    slab_i += 1
        b.op("sync", None, waits=st_all())
        dump_consts(0)

        pst_hist = []

        def emit_ln(u, from_fm, ps_tr_f, ps_tr_b, xload_marks=None,
                    src_marks=None):
            ln_ev = {}
            if from_fm:
                for g in range(4):
                    for dc in range(DC):
                        k = g * DC + dc
                        w = list(src_marks) if k == 0 else []
                        if k >= 2:
                            w.append(ln_ev[k - 2])
                        b.op("tensor", "transpose", ps_tr_f[k % 2][:],
                             u.x_f[:, dc * TILE + g * 128: dc * TILE + (g + 1) * 128],
                             ident_f[:], waits=w, inc=s_pe)
                        if k % 2 == 0:
                            b.op("vector", "tensor_copy",
                                 out=u.x_tm[:, g * D + dc * 128: g * D + (dc + 1) * 128],
                                 in_=ps_tr_f[0][:], waits=[(s_pe, b.c(s_pe))],
                                 inc=s_dve)
                            ln_ev[k] = (s_dve, b.c(s_dve))
                        else:
                            b.op("scalar", "copy",
                                 out=u.x_tm[:, g * D + dc * 128: g * D + (dc + 1) * 128],
                                 in_=ps_tr_f[1][:], waits=[(s_pe, b.c(s_pe))],
                                 inc=s_act)
                            ln_ev[k] = (s_act, b.c(s_act))
                tm_ready = [(s_dve, b.c(s_dve)), (s_act, b.c(s_act))]
                grp_ready = [tm_ready for _ in range(4)]
            else:
                grp_ready = [[xload_marks[g]] for g in range(4)]

            for g in range(4):
                b.op("scalar", "activation", out=u.sq_scr[:],
                     in_=u.x_tm[:, g * D:(g + 1) * D], func=AF.Square,
                     accum_out=ssq4[:, g:g + 1], waits=grp_ready[g], inc=s_act)
                b.op("vector", "reduce_sum", out=rs4[:, g:g + 1],
                     in_=u.x_tm[:, g * D:(g + 1) * D], axis=AXX,
                     waits=grp_ready[g], inc=s_dve)
            sq_done = b.c(s_act)
            b.op("vector", "tensor_scalar_mul", mean4[:], rs4[:], 1.0 / D,
                 waits=[(s_dve, b.c(s_dve))], inc=s_dve)
            b.op("vector", "tensor_scalar_mul", esq4[:], ssq4[:], 1.0 / D,
                 waits=[(s_act, sq_done)], inc=s_dve)
            b.op("vector", "tensor_mul", m24[:], mean4[:], mean4[:],
                 waits=[(s_dve, b.c(s_dve))], inc=s_dve)
            b.op("vector", "tensor_sub", var4[:], esq4[:], m24[:],
                 waits=[(s_dve, b.c(s_dve))], inc=s_dve)
            b.op("vector", "tensor_scalar_add", var4[:], var4[:], EPS,
                 waits=[(s_dve, b.c(s_dve))], inc=s_dve)
            b.op("vector", "reciprocal", rvar4[:], var4[:],
                 waits=[(s_dve, b.c(s_dve))], inc=s_dve)
            vd = b.c(s_dve)
            b.op("scalar", "activation", out=rstd4[:], in_=rvar4[:],
                 func=AF.Sqrt, waits=[(s_dve, vd)], inc=s_act)
            rstd_done = b.c(s_act)
            nrm_ev = {}
            for g in range(4):
                b.op("vector", "tensor_scalar", out=u.xn_tm[:],
                     in0=u.x_tm[:, g * D:(g + 1) * D],
                     scalar1=mean4[:, g:g + 1], scalar2=rstd4[:, g:g + 1],
                     op0=ALU.subtract, op1=ALU.mult,
                     waits=[(s_act, rstd_done)]
                     + ([nrm_ev[(g - 1) * DC + DC - 1]] if g > 0 else []),
                     inc=s_dve)
                nz = b.c(s_dve)
                for dc in range(DC):
                    k = g * DC + dc
                    w = [(s_dve, nz)]
                    if k >= 2:
                        w.append(nrm_ev[k - 2])
                    b.op("tensor", "transpose", ps_tr_b[k % 2][:],
                         u.xn_tm[:, dc * 128:(dc + 1) * 128], ident_b[:],
                         waits=w, inc=s_pe)
                    if k % 2 == 0:
                        b.op("vector", "tensor_copy",
                             out=u.xn_f[:, dc * TILE + g * 128: dc * TILE + (g + 1) * 128],
                             in_=ps_tr_b[0][:], waits=[(s_pe, b.c(s_pe))],
                             inc=s_dve)
                        nrm_ev[k] = (s_dve, b.c(s_dve))
                    else:
                        b.op("scalar", "copy",
                             out=u.xn_f[:, dc * TILE + g * 128: dc * TILE + (g + 1) * 128],
                             in_=ps_tr_b[1][:], waits=[(s_pe, b.c(s_pe))],
                             inc=s_act)
                        nrm_ev[k] = (s_act, b.c(s_act))
            return [(s_dve, b.c(s_dve)), (s_act, b.c(s_act))]

        def emit_rope_tables(u, t):
            for si in range(4):
                slot = t * 4 + si
                cs = slice(si * 128, (si + 1) * 128)
                b.op("vector", "tensor_scalar", out=u.rsc1[:, cs], in0=jtile[:],
                     scalar1=freq[:], scalar2=qf_all[:, slot:slot + 1],
                     op0=ALU.mult, op1=ALU.add,
                     inc=(s_dve if si == 3 else None))
            b.op("vector", "engine_nop", waits=[(s_dve, b.c(s_dve))])
            for (dst, shift) in ((u.rope_sp, 0.0), (u.rope_c, float(np.pi / 2))):
                if shift:
                    b.op("vector", "tensor_scalar_add", u.rsc1[:], u.rsc1[:],
                         shift, inc=s_dve)
                    b.op("vector", "engine_nop", waits=[(s_dve, b.c(s_dve))])
                b.op("vector", "tensor_scalar", out=u.rsc2[:], in0=u.rsc1[:],
                     scalar1=INV2PI, scalar2=RMAGIC, op0=ALU.mult, op1=ALU.add,
                     inc=s_dve)
                b.op("vector", "tensor_scalar_sub", u.rsc2[:], u.rsc2[:],
                     RMAGIC, waits=[(s_dve, b.c(s_dve))], inc=s_dve)
                b.op("vector", "scalar_tensor_tensor", out=dst[:], in0=u.rsc2[:],
                     scalar=-TWO_PI_HI, in1=u.rsc1[:], op0=ALU.mult,
                     op1=ALU.add, waits=[(s_dve, b.c(s_dve))], inc=s_dve)
                b.op("vector", "scalar_tensor_tensor", out=dst[:], in0=u.rsc2[:],
                     scalar=-TWO_PI_LO, in1=dst[:], op0=ALU.mult, op1=ALU.add,
                     waits=[(s_dve, b.c(s_dve))], inc=s_dve)
                b.op("vector", "tensor_scalar", out=dst[:], in0=dst[:],
                     scalar1=float(-np.pi), scalar2=float(np.pi),
                     op0=ALU.max, op1=ALU.min,
                     waits=[(s_dve, b.c(s_dve))], inc=s_dve)
                sv = b.c(s_dve)
                b.op("scalar", "activation", out=dst[:], in_=dst[:], func=AF.Sin,
                     waits=[(s_dve, sv)], inc=s_act)
            sa = b.c(s_act)
            b.op("vector", "tensor_scalar", out=u.rope_sp[:], in0=u.rope_sp[:],
                 scalar1=altsign[:], scalar2=None, op0=ALU.mult,
                 waits=[(s_act, sa)], inc=s_dve)
            return [(s_act, b.c(s_act)), (s_dve, b.c(s_dve))]

        def emit_proj(u, t, wsrc, nout, dst, rope_pred, ps_pj, xn_marks,
                      rope_marks):
            tc = slice(t * TILE, (t + 1) * TILE)
            pe0 = b.c(s_pe)
            pj_pe, pj_ev = {}, {}
            for oc in range(nout):
                par = oc % 2
                w = [(s_pe, pj_pe[oc - 2])] if oc >= 2 else [(s_pe, pe0)]
                b.op("sync", "dma_start",
                     out=u.wbig[:, par * HID: par * HID + D],
                     in_=wsrc.ap()[oc * 128:(oc + 1) * 128, :],
                     waits=w, inc=WPAR[par], incn=16)
                wl = b.c(WPAR[par])
                w = [(WPAR[par], wl)]
                if oc == 0:
                    w += xn_marks + rope_marks + adw()
                if oc >= 2:
                    w.append(pj_ev[oc - 2])
                for dc in range(DC):
                    b.op("tensor", "matmul", ps_pj[par][:],
                         u.wbig[:, par * HID + dc * 128: par * HID + (dc + 1) * 128],
                         u.xn_f[:, dc * TILE:(dc + 1) * TILE],
                         start=(dc == 0), stop=(dc == DC - 1),
                         waits=(w if dc == 0 else []),
                         inc=(s_pe if dc == DC - 1 else None))
                pj_pe[oc] = b.c(s_pe)
                idx = len(pst_hist)
                sg = idx % 4
                ssl = slice(sg * TILE, (sg + 1) * TILE)
                w = [(s_pe, pj_pe[oc])]
                if idx >= 4:
                    w.append(pst_hist[idx - 4])
                if rope_pred(oc):
                    b.op("vector", "tensor_copy",
                         out=u.silu_sb[:, par * TILE:(par + 1) * TILE],
                         in_=ps_pj[par][:], waits=w, inc=s_dve)
                    b.op("vector", "stream_shuffle", u.rsc1[:],
                         u.silu_sb[:, par * TILE:(par + 1) * TILE], SWAP32)
                    b.op("vector", "tensor_mul", u.rsc2[:],
                         u.silu_sb[:, par * TILE:(par + 1) * TILE], u.rope_c[:])
                    b.op("vector", "tensor_mul", u.rsc1[:], u.rsc1[:],
                         u.rope_sp[:])
                    b.op("vector", "tensor_add", u.pstage[:, ssl], u.rsc2[:],
                         u.rsc1[:], inc=s_dve)
                else:
                    b.op("vector", "tensor_copy", out=u.pstage[:, ssl],
                         in_=ps_pj[par][:], waits=w, inc=s_dve)
                pj_ev[oc] = (s_dve, b.c(s_dve))
                b.op("sync", "dma_start",
                     out=dst.ap()[oc * 128:(oc + 1) * 128, tc],
                     in_=u.pstage[:, ssl], waits=[pj_ev[oc]], inc=STS[2],
                     incn=16)
                pst_hist.append((STS[2], b.c(STS[2])))

        def emit_ffn(u, t, src_kind, src_dram, ps_tr_f, ps_tr_b, ps_ab, ps_dn):
            tc = slice(t * TILE, (t + 1) * TILE)
            if src_kind == "tm":
                xl = []
                for g in range(4):
                    r0 = t * TILE + g * 128
                    w = adw() + st_all() if g == 0 else []
                    b.op("sync", "dma_start", out=u.x_tm[:, g * D:(g + 1) * D],
                         in_=src_dram.ap()[r0:r0 + 128, :],
                         waits=w, inc=s_ldt, incn=16)
                    xl.append((s_ldt, b.c(s_ldt)))
                ln_marks = emit_ln(u, False, ps_tr_f, ps_tr_b, xload_marks=xl)
                xf_ev = {}
                for g in range(4):
                    for dc in range(DC):
                        k = g * DC + dc
                        w = [xl[g]]
                        if k >= 2:
                            w.append(xf_ev[k - 2])
                        b.op("tensor", "transpose", ps_tr_f[k % 2][:],
                             u.x_tm[:, g * D + dc * 128: g * D + (dc + 1) * 128],
                             ident_f[:], waits=w, inc=s_pe)
                        if k % 2 == 0:
                            b.op("vector", "tensor_copy",
                                 out=u.x_f[:, dc * TILE + g * 128: dc * TILE + (g + 1) * 128],
                                 in_=ps_tr_f[0][:], waits=[(s_pe, b.c(s_pe))],
                                 inc=s_dve)
                            xf_ev[k] = (s_dve, b.c(s_dve))
                        else:
                            b.op("scalar", "copy",
                                 out=u.x_f[:, dc * TILE + g * 128: dc * TILE + (g + 1) * 128],
                                 in_=ps_tr_f[1][:], waits=[(s_pe, b.c(s_pe))],
                                 inc=s_act)
                            xf_ev[k] = (s_act, b.c(s_act))
            else:
                for dc in range(DC):
                    w = adw() + st_all() if dc == 0 else []
                    b.op("sync", "dma_start",
                         out=u.x_f[:, dc * TILE:(dc + 1) * TILE],
                         in_=src_dram.ap()[dc * 128:(dc + 1) * 128, tc],
                         waits=w, inc=s_ldt, incn=16)
                xl0 = (s_ldt, b.c(s_ldt))
                ln_marks = emit_ln(u, True, ps_tr_f, ps_tr_b, src_marks=[xl0])

            pe0 = b.c(s_pe)
            f1_pe, f1_sil, f1_mul = {}, {}, {}
            for hc in range(HC):
                par = hc % 2
                w = [(s_pe, f1_pe[hc - 2])] if hc >= 2 else [(s_pe, pe0)]
                b.op("sync", "dma_start",
                     out=u.wbig[:, par * HID: par * HID + D],
                     in_=wg_t.ap()[hc * 128:(hc + 1) * 128, :],
                     waits=w, inc=WPAR[par], incn=16)
                b.op("sync", "dma_start",
                     out=u.wbig[:, par * HID + D: par * HID + 2 * D],
                     in_=wu_t.ap()[hc * 128:(hc + 1) * 128, :],
                     inc=WPAR[par], incn=16)
                wl = b.c(WPAR[par])
                w = [(WPAR[par], wl)]
                if hc == 0:
                    w += ln_marks + adw()
                if hc >= 2:
                    w.append((s_act, f1_sil[hc - 2]))
                    w.append((s_dve, f1_mul[hc - 2]))
                for dc in range(DC):
                    b.op("tensor", "matmul", ps_ab[par][:],
                         u.wbig[:, par * HID + dc * 128: par * HID + (dc + 1) * 128],
                         u.xn_f[:, dc * TILE:(dc + 1) * TILE],
                         start=(dc == 0), stop=(dc == DC - 1),
                         waits=(w if dc == 0 else []),
                         inc=(s_pe if dc == DC - 1 else None))
                geval = b.c(s_pe)
                for dc in range(DC):
                    b.op("tensor", "matmul", ps_ab[2 + par][:],
                         u.wbig[:, par * HID + D + dc * 128: par * HID + D + (dc + 1) * 128],
                         u.xn_f[:, dc * TILE:(dc + 1) * TILE],
                         start=(dc == 0), stop=(dc == DC - 1),
                         inc=(s_pe if dc == DC - 1 else None))
                f1_pe[hc] = b.c(s_pe)
                b.op("scalar", "activation",
                     out=u.silu_sb[:, par * TILE:(par + 1) * TILE],
                     in_=ps_ab[par][:], func=AF.Silu,
                     waits=[(s_pe, geval)], inc=s_act)
                f1_sil[hc] = b.c(s_act)
                b.op("vector", "tensor_mul",
                     u.gated[:, hc * TILE:(hc + 1) * TILE],
                     u.silu_sb[:, par * TILE:(par + 1) * TILE],
                     ps_ab[2 + par][:],
                     waits=[(s_act, f1_sil[hc]), (s_pe, f1_pe[hc])], inc=s_dve)
                f1_mul[hc] = b.c(s_dve)

            pe0 = b.c(s_pe)
            f2_pe, f2_ev = {}, {}
            for oc in range(DC):
                par = oc % 2
                w = [(s_pe, f2_pe[oc - 2])] if oc >= 2 else [(s_pe, pe0)]
                b.op("sync", "dma_start", out=u.wbig[:, par * HID:(par + 1) * HID],
                     in_=wd_t.ap()[oc * 128:(oc + 1) * 128, :],
                     waits=w, inc=WPAR[par], incn=16)
                wl = b.c(WPAR[par])
                w = [(WPAR[par], wl)]
                if oc == 0:
                    w += [(s_dve, b.c(s_dve))] + adw()
                if oc >= 2:
                    w.append(f2_ev[oc - 2])
                for hc in range(HC):
                    b.op("tensor", "matmul", ps_dn[par][:],
                         u.wbig[:, par * HID + hc * 128: par * HID + (hc + 1) * 128],
                         u.gated[:, hc * TILE:(hc + 1) * TILE],
                         start=(hc == 0), stop=(hc == HC - 1),
                         waits=(w if hc == 0 else []),
                         inc=(s_pe if hc == HC - 1 else None))
                f2_pe[oc] = b.c(s_pe)
                b.op("vector", "scalar_tensor_tensor",
                     out=u.x_f[:, oc * TILE:(oc + 1) * TILE],
                     in0=ps_dn[par][:], scalar=0.5,
                     in1=u.x_f[:, oc * TILE:(oc + 1) * TILE],
                     op0=ALU.mult, op1=ALU.add,
                     waits=[(s_pe, f2_pe[oc])], inc=s_dve)
                f2_ev[oc] = (s_dve, b.c(s_dve))
            return f2_ev

        if limit >= 1:
         with ExitStack() as p1:
            u = NS()
            u.x_tm = p1.enter_context(nc.sbuf_tensor("p1_xtm", [128, 4 * D], F32))
            u.xn_tm = p1.enter_context(nc.sbuf_tensor("p1_xntm", [128, D], BF16))
            u.x_f = p1.enter_context(nc.sbuf_tensor("p1_xf", [128, DC * TILE], F32))
            u.xn_f = p1.enter_context(nc.sbuf_tensor("p1_xnf", [128, DC * TILE], BF16))
            u.sq_scr = p1.enter_context(nc.sbuf_tensor("p1_sq", [128, D], BF16))
            u.gated = p1.enter_context(nc.sbuf_tensor("p1_gated", [128, HC * TILE], BF16))
            u.wbig = p1.enter_context(nc.sbuf_tensor("p1_wbig", [128, 2 * HID], BF16))
            u.silu_sb = p1.enter_context(nc.sbuf_tensor("p1_silu", [128, 2 * TILE], F32))
            u.pstage = p1.enter_context(nc.sbuf_tensor("p1_pst", [128, 4 * TILE], BF16))
            u.rope_c = p1.enter_context(nc.sbuf_tensor("p1_rc", [128, TILE], F32))
            u.rope_sp = p1.enter_context(nc.sbuf_tensor("p1_rsp", [128, TILE], F32))
            u.rsc1 = p1.enter_context(nc.sbuf_tensor("p1_rs1", [128, TILE], F32))
            u.rsc2 = p1.enter_context(nc.sbuf_tensor("p1_rs2", [128, TILE], F32))
            ps_tr_f = [p1.enter_context(nc.psum_tensor(f"p1_trf{i}", [128, 128], F32))
                       for i in range(2)]
            ps_tr_b = [p1.enter_context(nc.psum_tensor(f"p1_trb{i}", [128, 128], BF16))
                       for i in range(2)]
            ps_ab = [p1.enter_context(nc.psum_tensor(f"p1_ab{i}", [128, TILE], F32))
                     for i in range(4)]
            ps_dn = [ps_ab[0], ps_ab[1]]

            for t in range(NTILE):
                tc = slice(t * TILE, (t + 1) * TILE)
                res_marks = emit_ffn(u, t, "tm", x_d, ps_tr_f, ps_tr_b,
                                     ps_ab, ps_dn)
                if debug and t == 0:
                    dump_consts(2)
                for dc in range(DC):
                    b.op("sync", "dma_start",
                         out=x1_d.ap()[dc * 128:(dc + 1) * 128, tc],
                         in_=u.x_f[:, dc * TILE:(dc + 1) * TILE],
                         waits=[res_marks[dc]], inc=STS[2], incn=16)
                xn_marks = emit_ln(u, True, ps_tr_f, ps_tr_b,
                                   src_marks=[res_marks[DC - 1],
                                              res_marks[DC - 2]] + adw())
                if debug and t == 0:
                    dump_consts(3)
                rope_marks = emit_rope_tables(u, t)
                if debug:
                    b.op("sync", "dma_start",
                         out=dbg["ropec"].ap()[:, t * TILE:(t + 1) * TILE],
                         in_=u.rope_c[:], waits=rope_marks, inc=STS[5], incn=16)
                    b.op("sync", "dma_start",
                         out=dbg["ropesp"].ap()[:, t * TILE:(t + 1) * TILE],
                         in_=u.rope_sp[:], waits=rope_marks, inc=STS[5], incn=16)
                emit_proj(u, t, kvw_t, 8, kvag_in, lambda oc: oc < 4,
                          [ps_ab[2], ps_ab[3]], xn_marks, rope_marks)
                if debug and t == 0:
                    dump_consts(4)
                emit_proj(u, t, qw_t, DC, q_d, lambda oc: True,
                          [ps_ab[2], ps_ab[3]], xn_marks, rope_marks)
                if debug and t == 0:
                    dump_consts(5)
                if debug and t == 1:
                    dump_consts(6)

        if limit >= 2:
          b.op("gpsimd", "collective_compute", "AllGather", ALU.bypass,
             replica_groups=(cc_groups or [[0, 1], [2, 3], [4, 5], [6, 7]]),
             ins=[kvag_in.ap()], outs=[kvag_out.ap()],
             waits=st_all(), inc=s_cc)

        def owner(g):
            if g <= 15:
                return (g % 2, g if g % 2 == 0 else g - 1)
            return (1 - g % 2, 31 - g if g % 2 == 0 else 32 - g)

        if limit >= 3:
         with ExitStack() as p4:
            a = NS()
            a.kf = p4.enter_context(nc.sbuf_tensor("a_kf", [128, HKV * T], BF16))
            a.vt = p4.enter_context(nc.sbuf_tensor("a_vt", [128, HKV * T], BF16))
            a.qg = p4.enter_context(nc.sbuf_tensor("a_qg", [128, HQ * TILE], BF16))
            a.attn = p4.enter_context(nc.sbuf_tensor("a_attn", [128, HQ * TILE], BF16))
            a.srow = p4.enter_context(nc.sbuf_tensor("a_srow", [128, T], F32))
            a.p4 = p4.enter_context(nc.sbuf_tensor("a_p4", [128, 4 * T], BF16))
            a.ptsb = p4.enter_context(nc.sbuf_tensor("a_ptsb", [128, 2 * 512], BF16))
            a.masks = p4.enter_context(nc.sbuf_tensor("a_masks", [128, 4 * KB], F32))
            a.resld = p4.enter_context(nc.sbuf_tensor("a_res", [128, 2 * TILE], F32))
            a.x2buf = p4.enter_context(nc.sbuf_tensor("a_x2", [128, 2 * TILE], F32))
            a.cwb = p4.enter_context(nc.sbuf_tensor("a_cwb", [128, 2 * D], BF16))
            a.vstage = p4.enter_context(nc.sbuf_tensor("a_vst", [128, 2 * 128], BF16))

            with ExitStack() as p3:
                ps_vtr = [p3.enter_context(nc.psum_tensor(f"a_vtr{i}", [128, 128], BF16))
                          for i in range(2)]
                v_pe, v_ev = {}, {}
                for h in range(HKV):
                    for g in range(32):
                        r, s = owner(g)
                        b.op("sync", "dma_start",
                             out=a.kf[:, h * T + g * 128: h * T + (g + 1) * 128],
                             in_=kvag_out.ap()[r * 1024 + h * 128:
                                               r * 1024 + (h + 1) * 128,
                                               s * 128:(s + 1) * 128],
                             waits=[(s_cc, 1)], inc=s_ldt, incn=16)
                        k = h * 32 + g
                        w = [(s_cc, 1)]
                        if k >= 2:
                            w.append((s_pe, v_pe[k - 2]))
                        b.op("sync", "dma_start",
                             out=a.vstage[:, (k % 2) * 128:(k % 2 + 1) * 128],
                             in_=kvag_out.ap()[r * 1024 + 512 + h * 128:
                                               r * 1024 + 512 + (h + 1) * 128,
                                               s * 128:(s + 1) * 128],
                             waits=w, inc=VPAR[k % 2], incn=16)
                        w = [(VPAR[k % 2], b.c(VPAR[k % 2]))]
                        if k == 0:
                            w += adw()
                        if k >= 2:
                            w.append(v_ev[k - 2])
                        b.op("tensor", "transpose", ps_vtr[k % 2][:],
                             a.vstage[:, (k % 2) * 128:(k % 2 + 1) * 128],
                             ident_b[:], waits=w, inc=s_pe)
                        v_pe[k] = b.c(s_pe)
                        if k % 2 == 0:
                            b.op("vector", "tensor_copy",
                                 out=a.vt[:, k * 128:(k + 1) * 128],
                                 in_=ps_vtr[0][:], waits=[(s_pe, v_pe[k])],
                                 inc=s_dve)
                            v_ev[k] = (s_dve, b.c(s_dve))
                        else:
                            b.op("scalar", "copy",
                                 out=a.vt[:, k * 128:(k + 1) * 128],
                                 in_=ps_vtr[1][:], waits=[(s_pe, v_pe[k])],
                                 inc=s_act)
                            v_ev[k] = (s_act, b.c(s_act))
                kv_ready = [(s_ldt, b.c(s_ldt)), (s_dve, b.c(s_dve)),
                            (s_act, b.c(s_act))]

            for t in (range(NTILE) if limit >= 4 else []):
                tc = slice(t * TILE, (t + 1) * TILE)
                for hq in range(HQ):
                    w = kv_ready + [(s_pe, b.c(s_pe))] if hq == 0 else []
                    b.op("sync", "dma_start",
                         out=a.qg[:, hq * TILE:(hq + 1) * TILE],
                         in_=q_d.ap()[hq * 128:(hq + 1) * 128, tc],
                         waits=w, inc=s_ldt, incn=16)
                q_ready = (s_ldt, b.c(s_ldt))
                for si in range(4):
                    slot = t * 4 + si
                    ms = slice(si * KB, (si + 1) * KB)
                    b.op("gpsimd", "tensor_scalar", out=a.masks[:, ms],
                         in0=dtile[:], scalar1=qpos_cols[:, slot:slot + 1],
                         scalar2=float(512.0 * (EXT[slot] - 1)),
                         op0=ALU.subtract, op1=ALU.add,
                         waits=(adw() if si == 0 else []))
                    b.op("gpsimd", "tensor_single_scalar", out=a.masks[:, ms],
                         in_=a.masks[:, ms], scalar=0.0, op=ALU.is_gt)
                    b.op("gpsimd", "tensor_scalar_mul", a.masks[:, ms],
                         a.masks[:, ms], MASKVAL, inc=s_pool)
                mask_ready = (s_pool, b.c(s_pool))

                with ExitStack() as pi:
                    if alimit < 1:
                        continue
                    ps_s = [pi.enter_context(nc.psum_tensor(f"a_s{t}{i}", [128, KB], F32))
                            for i in range(2)]
                    ps_pt = [pi.enter_context(nc.psum_tensor(f"a_pt{t}{i}", [128, 128], BF16))
                             for i in range(4)]
                    ps_pv = pi.enter_context(nc.psum_tensor(f"a_pv{t}", [128, 512], F32))
                    sbank = [None, None]
                    ptlast = None
                    pvlast = None
                    first = True
                    for si in range(4):
                        slot = t * 4 + si
                        e = EXT[slot]
                        nch = 4 * e
                        for kvh in range(HKV):
                            for qh in range(4):
                                hq = kvh * 4 + qh
                                for blk in range(e):
                                    bk = blk % 2
                                    w = []
                                    if first:
                                        w += [q_ready, mask_ready] + adw()
                                        first = False
                                    if sbank[bk] is not None:
                                        w.append(sbank[bk])
                                    b.op("tensor", "matmul", ps_s[bk][:],
                                         a.qg[:, hq * TILE + si * 128:
                                              hq * TILE + (si + 1) * 128],
                                         a.kf[:, kvh * T + blk * KB:
                                              kvh * T + (blk + 1) * KB],
                                         start=True, stop=True, waits=w,
                                         inc=s_pe)
                                    pm = b.c(s_pe)
                                    if blk == e - 1:
                                        b.op("vector", "tensor_tensor",
                                             out=a.srow[:, blk * KB:(blk + 1) * KB],
                                             in0=ps_s[bk][:],
                                             in1=a.masks[:, si * KB:(si + 1) * KB],
                                             op=ALU.add,
                                             waits=[(s_pe, pm),
                                                    (s_act, b.c(s_act))],
                                             inc=s_dve)
                                        sbank[bk] = (s_dve, b.c(s_dve))
                                    else:
                                        b.op("scalar", "copy",
                                             out=a.srow[:, blk * KB:(blk + 1) * KB],
                                             in_=ps_s[bk][:],
                                             waits=[(s_pe, pm)], inc=s_act)
                                        sbank[bk] = (s_act, b.c(s_act))
                                evw = [sbank[0], sbank[1],
                                       (s_pe, b.c(s_pe))]
                                evw = [x for x in evw if x is not None]
                                b.op("scalar", "activation",
                                     out=a.p4[:, qh * T: qh * T + e * KB],
                                     in_=a.srow[:, 0:e * KB], func=AF.Exp,
                                     bias=negB[:], scale=SCALE,
                                     accum_out=sums4[:, qh:qh + 1],
                                     waits=evw, inc=s_act)
                            exp_done = b.c(s_act)
                            if alimit < 2:
                                continue
                            b.op("vector", "reciprocal", recip4[:], sums4[:],
                                 waits=[(s_act, exp_done)], inc=s_dve)
                            rc_done = b.c(s_dve)
                            for qh in range(4):
                                b.op("gpsimd", "tensor_scalar",
                                     out=a.p4[:, qh * T: qh * T + e * KB],
                                     in0=a.p4[:, qh * T: qh * T + e * KB],
                                     scalar1=recip4[:, qh:qh + 1], scalar2=None,
                                     op0=ALU.mult, waits=[(s_dve, rc_done)],
                                     inc=s_pool)
                            pnorm = b.c(s_pool)
                            if alimit < 3:
                                continue
                            for c in range(nch):
                                w = [(s_pool, pnorm)] if c == 0 else []
                                if c == 0 and pvlast is not None:
                                    w += pvlast
                                if ptlast is not None:
                                    w += ptlast
                                for qh in range(4):
                                    b.op("tensor", "transpose", ps_pt[qh][:],
                                         a.p4[:, qh * T + c * 128:
                                              qh * T + (c + 1) * 128],
                                         ident_b[:],
                                         waits=(w if qh == 0 else []),
                                         inc=(s_pe if qh == 3 else None))
                                trm = b.c(s_pe)
                                cs = (c % 2) * 512
                                for qh in range(4):
                                    if qh % 2 == 0:
                                        b.op("vector", "tensor_copy",
                                             out=a.ptsb[:, cs + qh * 128:
                                                        cs + (qh + 1) * 128],
                                             in_=ps_pt[qh][:],
                                             waits=[(s_pe, trm)], inc=s_dve)
                                    else:
                                        b.op("scalar", "copy",
                                             out=a.ptsb[:, cs + qh * 128:
                                                        cs + (qh + 1) * 128],
                                             in_=ps_pt[qh][:],
                                             waits=[(s_pe, trm)], inc=s_act)
                                ptlast = [(s_dve, b.c(s_dve)),
                                          (s_act, b.c(s_act))]
                                b.op("tensor", "matmul", ps_pv[:],
                                     a.vt[:, (kvh * 32 + c) * 128:
                                          (kvh * 32 + c + 1) * 128],
                                     a.ptsb[:, cs:cs + 512],
                                     start=(c == 0), stop=(c == nch - 1),
                                     waits=ptlast,
                                     inc=(s_pe if c == nch - 1 else None))
                            pv_done = b.c(s_pe)
                            oc0 = (4 * kvh) * TILE + si * 128
                            b.op("vector", "tensor_copy",
                                 out=a.attn[:, :].rearrange(
                                     "p (h u) -> p h u", h=HQ)[
                                     :, 4 * kvh:4 * kvh + 4,
                                     si * 128:(si + 1) * 128],
                                 in_=ps_pv[:, :].rearrange(
                                     "p (h u) -> p h u", h=4),
                                 waits=[(s_pe, pv_done)], inc=s_dve)
                            pvlast = [(s_dve, b.c(s_dve))]
                    attn_all = [(s_dve, b.c(s_dve)), (s_act, b.c(s_act))]

                if alimit < 4:
                    continue
                with ExitStack() as pc:
                    ps_c = [pc.enter_context(nc.psum_tensor(f"a_c{t}{i}", [128, TILE], F32))
                            for i in range(2)]
                    pe0 = b.c(s_pe)
                    cp_pe, cp_ev, cp_st = {}, {}, {}
                    for oc in range(DC):
                        par = oc % 2
                        w = [(s_pe, cp_pe[oc - 2])] if oc >= 2 else [(s_pe, pe0)]
                        b.op("sync", "dma_start",
                             out=a.cwb[:, par * D:(par + 1) * D],
                             in_=cw_t.ap()[oc * 128:(oc + 1) * 128, :],
                             waits=w, inc=WPAR[par], incn=16)
                        wl = b.c(WPAR[par])
                        w2 = [cp_ev[oc - 2]] if oc >= 2 else []
                        b.op("sync", "dma_start",
                             out=a.resld[:, par * TILE:(par + 1) * TILE],
                             in_=x1_d.ap()[oc * 128:(oc + 1) * 128, tc],
                             waits=w2, inc=VPAR[par], incn=16)
                        rl = b.c(VPAR[par])
                        w = [(WPAR[par], wl)]
                        if oc == 0:
                            w += attn_all + adw()
                        if oc >= 2:
                            w.append(cp_ev[oc - 2])
                        for dc in range(DC):
                            b.op("tensor", "matmul", ps_c[par][:],
                                 a.cwb[:, par * D + dc * 128: par * D + (dc + 1) * 128],
                                 a.attn[:, dc * TILE:(dc + 1) * TILE],
                                 start=(dc == 0), stop=(dc == DC - 1),
                                 waits=(w if dc == 0 else []),
                                 inc=(s_pe if dc == DC - 1 else None))
                        cp_pe[oc] = b.c(s_pe)
                        w = [(s_pe, cp_pe[oc]), (VPAR[par], rl)]
                        if oc >= 2:
                            w.append(cp_st[oc - 2])
                        b.op("vector", "tensor_add",
                             a.x2buf[:, par * TILE:(par + 1) * TILE],
                             ps_c[par][:], a.resld[:, par * TILE:(par + 1) * TILE],
                             waits=w, inc=s_dve)
                        cp_ev[oc] = (s_dve, b.c(s_dve))
                        b.op("sync", "dma_start",
                             out=x2_d.ap()[oc * 128:(oc + 1) * 128, tc],
                             in_=a.x2buf[:, par * TILE:(par + 1) * TILE],
                             waits=[cp_ev[oc]], inc=STS[3], incn=16)
                        cp_st[oc] = (STS[3], b.c(STS[3]))

        if limit >= 5:
         with ExitStack() as p5:
            u = NS()
            u.x_tm = p5.enter_context(nc.sbuf_tensor("p5_xtm", [128, 4 * D], F32))
            u.xn_tm = p5.enter_context(nc.sbuf_tensor("p5_xntm", [128, D], BF16))
            u.x_f = p5.enter_context(nc.sbuf_tensor("p5_xf", [128, DC * TILE], F32))
            u.xn_f = p5.enter_context(nc.sbuf_tensor("p5_xnf", [128, DC * TILE], BF16))
            u.sq_scr = p5.enter_context(nc.sbuf_tensor("p5_sq", [128, D], BF16))
            u.gated = p5.enter_context(nc.sbuf_tensor("p5_gated", [128, HC * TILE], BF16))
            u.wbig = p5.enter_context(nc.sbuf_tensor("p5_wbig", [128, 2 * HID], BF16))
            u.silu_sb = p5.enter_context(nc.sbuf_tensor("p5_silu", [128, 2 * TILE], F32))
            u.stage = p5.enter_context(nc.sbuf_tensor("p5_stage", [128, 8 * 128], F32))
            ps_tr_f = [p5.enter_context(nc.psum_tensor(f"p5_trf{i}", [128, 128], F32))
                       for i in range(2)]
            ps_tr_b = [p5.enter_context(nc.psum_tensor(f"p5_trb{i}", [128, 128], BF16))
                       for i in range(2)]
            ps_ab = [p5.enter_context(nc.psum_tensor(f"p5_ab{i}", [128, TILE], F32))
                     for i in range(4)]
            ps_dn = [ps_ab[0], ps_ab[1]]

            for t in range(NTILE):
                res_marks = emit_ffn(u, t, "fm", x2_d, ps_tr_f, ps_tr_b,
                                     ps_ab, ps_dn)
                if debug:
                    if t == 0:
                        b.op("sync", "dma_start", out=dbg["xn5"].ap(),
                             in_=u.xn_f[:], waits=[(s_dve, b.c(s_dve)),
                                                   (s_act, b.c(s_act))],
                             inc=STS[5], incn=16)
                    for dc in range(DC):
                        b.op("sync", "dma_start",
                             out=dbg["x3"].ap()[dc * 128:(dc + 1) * 128,
                                                t * TILE:(t + 1) * TILE],
                             in_=u.x_f[:, dc * TILE:(dc + 1) * TILE],
                             waits=[res_marks[dc]], inc=STS[5], incn=16)
                o_ev, o_st = {}, {}
                for dc in range(DC):
                    for g in range(4):
                        k = dc * 4 + g
                        w = [res_marks[dc]]
                        if k >= 2:
                            w.append(o_ev[k - 2])
                        b.op("tensor", "transpose", ps_tr_f[k % 2][:],
                             u.x_f[:, dc * TILE + g * 128: dc * TILE + (g + 1) * 128],
                             ident_f[:], waits=w, inc=s_pe)
                        sg = k % 8
                        ew = [(s_pe, b.c(s_pe))]
                        if k >= 8:
                            ew.append(o_st[k - 8])
                        if k % 2 == 0:
                            b.op("vector", "tensor_copy",
                                 out=u.stage[:, sg * 128:(sg + 1) * 128],
                                 in_=ps_tr_f[0][:], waits=ew, inc=s_dve)
                            o_ev[k] = (s_dve, b.c(s_dve))
                        else:
                            b.op("scalar", "copy",
                                 out=u.stage[:, sg * 128:(sg + 1) * 128],
                                 in_=ps_tr_f[1][:], waits=ew, inc=s_act)
                            o_ev[k] = (s_act, b.c(s_act))
                        w = [o_ev[k]]
                        b.op("sync", "dma_start",
                             out=out_d.ap()[t * TILE + g * 128:
                                            t * TILE + (g + 1) * 128,
                                            dc * 128:(dc + 1) * 128],
                             in_=u.stage[:, sg * 128:(sg + 1) * 128],
                             waits=w, inc=STS[4], incn=16)
                        o_st[k] = (STS[4], b.c(STS[4]))

        if debug:
            pairs = [(dbg["x1"], x1_d), (dbg["x2"], x2_d), (dbg["q"], q_d),
                     (dbg["kv"], kvag_out)]
            for dd, ss in pairs:
                b.op("sync", "dma_start", out=dd.ap(), in_=ss.ap(),
                     waits=st_all() + adw(), inc=STS[5], incn=16)

        b.op("sync", None, waits=st_all())

        @block.sync
        def _(e):
            b.realize_engine(e, "sync")

        @block.vector
        def _(e):
            b.realize_engine(e, "vector")

        @block.scalar
        def _(e):
            b.realize_engine(e, "scalar")

        @block.tensor
        def _(e):
            b.realize_engine(e, "tensor")

        @block.gpsimd
        def _(e):
            b.realize_engine(e, "gpsimd")

    return nc


def _host_shards(x):
    rows = [np.concatenate([np.arange(g * 128, (g + 1) * 128) for g in SLOTS[r]])
            for r in range(2)]
    shards, metas = [], []
    for c in range(8):
        bidx, r = c // 2, c % 2
        shards.append(np.ascontiguousarray(x[bidx][rows[r]], dtype=np.float32))
        metas.append(np.array([[128.0 * g for g in SLOTS[r]]], dtype=np.float32))
    return shards, metas, rows


def kernel(L0_attn_x, L0_attn_q_w, L0_attn_kv_w, L0_attn_c_w,
           L0_ffn_w_gate, L0_ffn_w_up, L0_ffn_w_down):
    x = np.asarray(L0_attn_x, dtype=np.float32)
    shards, metas, rows = _host_shards(x)
    weights = {
        "qw": np.ascontiguousarray(L0_attn_q_w, dtype=np.float32),
        "kvw": np.ascontiguousarray(L0_attn_kv_w, dtype=np.float32),
        "cw": np.ascontiguousarray(L0_attn_c_w, dtype=np.float32),
        "wg": np.ascontiguousarray(L0_ffn_w_gate, dtype=np.float32),
        "wu": np.ascontiguousarray(L0_ffn_w_up, dtype=np.float32),
        "wd": np.ascontiguousarray(L0_ffn_w_down, dtype=np.float32),
    }
    nc = build_nc()
    nc.finalize()
    in_maps = [dict(weights, x=shards[c], qpos=metas[c]) for c in range(8)]
    res = run_bass_kernel_spmd(nc, in_maps, core_ids=list(range(8)))
    out = np.empty((4, 4096, 2048), dtype=np.float32)
    for c in range(8):
        out[c // 2][rows[c % 2]] = res.results[c]["out"]
    return out


# version 05
# speedup vs baseline: 1.5597x; 1.5597x over previous
import numpy as np
from contextlib import ExitStack
import concourse.bass as bass
import concourse.mybir as mybir
from concourse.bass_utils import run_bass_kernel_spmd

F32 = mybir.dt.float32
BF16 = mybir.dt.bfloat16
AF = mybir.ActivationFunctionType
ALU = mybir.AluOpType
AXX = mybir.AxisListType.X

D = 2048
DC = 16
HID = 8192
HC = 64
NT = 2048
T = 4096
TILE = 512
NTILE = 4
HQ = 16
HKV = 4
KB = 512
SCALE = float(1.0 / np.sqrt(128.0))
BSHIFT = -40.0
MASKVAL = -1e30
EPS = 1e-5
RMAGIC = 12582912.0
TWO_PI_HI = 6.28125
TWO_PI_LO = 0.0019353071795864769
LN_THETA = float(np.log(10000.0))
INV2PI = float(1.0 / (2 * np.pi))

SLOTS = [[], []]
for _i in range(8):
    SLOTS[0] += [2 * _i, 31 - 2 * _i]
    SLOTS[1] += [2 * _i + 1, 30 - 2 * _i]
EXT = [g // 4 + 1 for g in SLOTS[0]]
assert EXT == [g // 4 + 1 for g in SLOTS[1]]

ENGS = ("sync", "vector", "scalar", "tensor", "gpsimd")
SWAP32 = [i ^ 1 for i in range(32)]


class Builder:

    def __init__(self):
        self.st = {e: [] for e in ENGS}
        self.cnt = {}

    def op(self, eng, meth, *a, inc=None, incn=1, waits=(), **kw):
        self.st[eng].append((tuple(waits), meth, a, kw, inc, incn))
        if inc is not None:
            nm = inc.name
            self.cnt[nm] = self.cnt.get(nm, 0) + incn
            return self.cnt[nm]
        return None

    def c(self, sem):
        return self.cnt.get(sem.name, 0)

    def realize_engine(self, engobj, eng):
        seen = {}
        for waits, meth, a, kw, inc, incn in self.st[eng]:
            for semh, val in waits:
                if val <= 0 or seen.get(semh.name, -1) >= val:
                    continue
                seen[semh.name] = val
                engobj.wait_ge(semh, val)
            if meth is None:
                continue
            ins = getattr(engobj, meth)(*a, **kw)
            if inc is not None:
                ins.then_inc(inc, incn)


class NS:
    pass


def build_nc(debug=False, cc_groups=None, limit=5, alimit=4):
    nc = bass.Bass("TRN2", target_bir_lowering=False,
                   detect_race_conditions=False)

    x_d = nc.dram_tensor("x", [NT, D], F32, kind="ExternalInput")
    qw_d = nc.dram_tensor("qw", [D, D], F32, kind="ExternalInput")
    kvw_d = nc.dram_tensor("kvw", [1024, D], F32, kind="ExternalInput")
    cw_d = nc.dram_tensor("cw", [D, D], F32, kind="ExternalInput")
    wg_d = nc.dram_tensor("wg", [HID, D], F32, kind="ExternalInput")
    wu_d = nc.dram_tensor("wu", [HID, D], F32, kind="ExternalInput")
    wd_d = nc.dram_tensor("wd", [D, HID], F32, kind="ExternalInput")
    qpos_d = nc.dram_tensor("qpos", [1, 16], F32, kind="ExternalInput")
    out_d = nc.dram_tensor("out", [NT, D], F32, kind="ExternalOutput")

    wg_t = nc.dram_tensor("wg_t", [HID, D], BF16)
    wu_t = nc.dram_tensor("wu_t", [HID, D], BF16)
    wd_t = nc.dram_tensor("wd_t", [D, HID], BF16)
    qw_t = nc.dram_tensor("qw_t", [D, D], BF16)
    kvw_t = nc.dram_tensor("kvw_t", [1024, D], BF16)
    cw_t = nc.dram_tensor("cw_t", [D, D], BF16)
    x1_d = nc.dram_tensor("x1", [D, NT], F32)
    x2_d = nc.dram_tensor("x2", [D, NT], F32)
    q_d = nc.dram_tensor("qbuf", [D, NT], BF16)
    kvag_in = nc.dram_tensor("kvag_in", [1024, NT], BF16)
    kvag_out = nc.dram_tensor("kvag_out", [2048, NT], BF16)
    if debug:
        dbg = {nm: nc.dram_tensor("dbg_" + nm, [D, NT], F32, kind="ExternalOutput")
               for nm in ("x1", "x2")}
        dbg["q"] = nc.dram_tensor("dbg_q", [D, NT], BF16, kind="ExternalOutput")
        dbg["x3"] = nc.dram_tensor("dbg_x3", [D, NT], F32, kind="ExternalOutput")
        dbg["xn5"] = nc.dram_tensor("dbg_xn5", [128, DC * TILE], BF16, kind="ExternalOutput")
        dbg["ropec"] = nc.dram_tensor("dbg_ropec", [128, NTILE * TILE], F32, kind="ExternalOutput")
        dbg["consts"] = nc.dram_tensor("dbg_consts", [128, 10 * 168], F32, kind="ExternalOutput")
        dbg["ropesp"] = nc.dram_tensor("dbg_ropesp", [128, NTILE * TILE], F32, kind="ExternalOutput")
        dbg["kv"] = nc.dram_tensor("dbg_kv", [2048, NT], BF16, kind="ExternalOutput")

    b = Builder()
    es = ExitStack()
    with es:
        sbuf = lambda name, shape, dt: es.enter_context(nc.sbuf_tensor(name, shape, dt))
        sem = lambda name: es.enter_context(nc.semaphore(name))

        s_pe = sem("s_pe")
        s_dve = sem("s_dve")
        s_act = sem("s_act")
        s_pool = sem("s_pool")
        s_w0 = sem("s_w0")
        s_w1 = sem("s_w1")
        s_v0 = sem("s_v0")
        s_v1 = sem("s_v1")
        s_ldt = sem("s_ldt")
        STS = [sem(f"s_st{i}") for i in range(6)]
        s_cc = sem("s_cc")
        WPAR = (s_w0, s_w1)
        VPAR = (s_v0, s_v1)

        def adw():
            return [(s_pe, b.c(s_pe)), (s_dve, b.c(s_dve)),
                    (s_act, b.c(s_act)), (s_pool, b.c(s_pool))]

        def st_all():
            return [(s, b.c(s)) for s in STS]

        ident_f = sbuf("ident_f", [128, 128], F32)
        ident_b = sbuf("ident_b", [128, 128], BF16)
        jtile = sbuf("jtile", [128, 128], F32)
        dtile = sbuf("dtile", [128, KB], F32)
        iota_p = sbuf("iota_p", [128, 1], F32)
        halfp = sbuf("halfp", [128, 1], F32)
        freq = sbuf("freq", [128, 1], F32)
        altsign = sbuf("altsign", [128, 1], F32)
        negB = sbuf("negB", [128, 1], F32)
        ones_row = sbuf("ones_row", [1, 128], F32)
        qpos_sb = sbuf("qpos_sb", [1, 16], F32)
        qpos_cols = sbuf("qpos_cols", [128, 16], F32)
        qf_all = sbuf("qf_all", [128, 16], F32)
        rs4 = sbuf("rs4", [128, 4], F32)
        ssq4 = sbuf("ssq4", [128, 4], F32)
        mean4 = sbuf("mean4", [128, 4], F32)
        esq4 = sbuf("esq4", [128, 4], F32)
        m24 = sbuf("m24", [128, 4], F32)
        var4 = sbuf("var4", [128, 4], F32)
        rvar4 = sbuf("rvar4", [128, 4], F32)
        rstd4 = sbuf("rstd4", [128, 4], F32)
        sums4 = sbuf("sums4", [128, 4], F32)
        recip4 = sbuf("recip4", [128, 4], F32)

        block = es.enter_context(nc.Block())

        def dump_consts(half):
            if not debug:
                return
            col = half * 168
            srcs = [(iota_p, 1), (halfp, 1), (freq, 1), (altsign, 1), (negB, 1),
                    (qf_all, 16), (qpos_cols, 16), (jtile, 128), (dtile, 3)]
            for tsr, wdt in srcs:
                b.op("sync", "dma_start",
                     out=dbg["consts"].ap()[:, col:col + wdt],
                     in_=tsr[:, 0:wdt], waits=adw(), inc=STS[5], incn=16,
                     allow_slow_non_contiguous=True)
                col += wdt

        b.op("gpsimd", "iota", ident_f[:], pattern=[[0, 128]], base=0,
             channel_multiplier=1, allow_small_or_imprecise_dtypes=True)
        b.op("gpsimd", "iota", jtile[:], pattern=[[1, 128]], base=0,
             channel_multiplier=0, allow_small_or_imprecise_dtypes=True)
        b.op("gpsimd", "iota", dtile[:], pattern=[[1, KB]], base=0,
             channel_multiplier=-1, allow_small_or_imprecise_dtypes=True)
        b.op("gpsimd", "iota", iota_p[:], pattern=[[0, 1]], base=0,
             channel_multiplier=1, allow_small_or_imprecise_dtypes=True,
             inc=s_pool)
        g0 = b.c(s_pool)
        b.op("vector", "tensor_tensor", ident_f[:], ident_f[:], jtile[:],
             ALU.is_equal, waits=[(s_pool, g0)], inc=s_dve)
        b.op("vector", "tensor_copy", out=ident_b[:], in_=ident_f[:],
             waits=[(s_dve, b.c(s_dve))], inc=s_dve)
        b.op("vector", "tensor_scalar", out=halfp[:], in0=iota_p[:],
             scalar1=0.5, scalar2=0.25, op0=ALU.mult, op1=ALU.subtract,
             inc=s_dve)
        b.op("vector", "tensor_scalar", out=halfp[:], in0=halfp[:],
             scalar1=RMAGIC, scalar2=RMAGIC, op0=ALU.add, op1=ALU.subtract,
             waits=[(s_dve, b.c(s_dve))], inc=s_dve)
        b.op("vector", "tensor_scalar_mul", freq[:], halfp[:],
             -2.0 * LN_THETA / 128.0, waits=[(s_dve, b.c(s_dve))], inc=s_dve)
        fq = b.c(s_dve)
        b.op("scalar", "activation", out=freq[:], in_=freq[:], func=AF.Exp,
             waits=[(s_dve, fq)], inc=s_act)
        fq_a = b.c(s_act)
        b.op("vector", "tensor_scalar_mul", altsign[:], halfp[:], -2.0,
             inc=s_dve)
        b.op("vector", "tensor_add", altsign[:], altsign[:], iota_p[:],
             waits=[(s_dve, b.c(s_dve))], inc=s_dve)
        b.op("vector", "tensor_scalar", out=altsign[:], in0=altsign[:],
             scalar1=2.0, scalar2=-1.0, op0=ALU.mult, op1=ALU.add,
             waits=[(s_dve, b.c(s_dve))], inc=s_dve)
        b.op("vector", "memset", negB[:], BSHIFT)
        b.op("vector", "memset", ones_row[:], 1.0, inc=s_dve)
        consts_v = b.c(s_dve)

        b.op("sync", "dma_start", out=qpos_sb[:], in_=qpos_d.ap(),
             inc=s_ldt, incn=16)
        qpos_ld = b.c(s_ldt)

        prep_list = [
            (wg_d, wg_t, HC, DC),
            (wu_d, wu_t, HC, DC),
            (wd_d, wd_t, DC, HC),
            (qw_d, qw_t, DC, DC),
            (kvw_d, kvw_t, 8, DC),
            (cw_d, cw_t, DC, DC),
        ]
        with ExitStack() as pes:
            ps_qposb = pes.enter_context(nc.psum_tensor("ps_qposb", [128, 16], F32))
            ps_prep = [pes.enter_context(nc.psum_tensor(f"ps_prep{i}", [128, 128], BF16))
                       for i in range(4)]
            slab_f = pes.enter_context(nc.sbuf_tensor("slab_f", [128, 2 * HID], F32))
            slab_b = pes.enter_context(nc.sbuf_tensor("slab_b", [128, 2 * HID], BF16))
            tout = pes.enter_context(nc.sbuf_tensor("tout", [128, 2 * HID], BF16))

            b.op("tensor", "matmul", ps_qposb[:], ones_row[:], qpos_sb[:],
                 start=True, stop=True,
                 waits=[(s_ldt, qpos_ld), (s_dve, consts_v)], inc=s_pe)
            b.op("vector", "tensor_copy", out=qpos_cols[:], in_=ps_qposb[:],
                 waits=[(s_pe, b.c(s_pe))], inc=s_dve)
            b.op("vector", "tensor_scalar", out=qf_all[:], in0=qpos_cols[:],
                 scalar1=freq[:], scalar2=None, op0=ALU.mult,
                 waits=[(s_act, fq_a), (s_dve, b.c(s_dve))], inc=s_dve)

            slab_i = 0
            tcount = 0
            conv_marks, tr_marks, evac_marks, st_marks = {}, {}, {}, {}
            for src, dst, nrc, ncc in prep_list:
                cols = ncc * 128
                for rc in range(nrc):
                    par = slab_i % 2
                    w = []
                    if slab_i >= 2:
                        w.append((s_dve, conv_marks[slab_i - 2]))
                    b.op("sync", "dma_start",
                         out=slab_f[:, par * HID: par * HID + cols],
                         in_=src.ap()[rc * 128:(rc + 1) * 128, :],
                         waits=w, inc=WPAR[par], incn=16)
                    ldv = b.c(WPAR[par])
                    w = [(WPAR[par], ldv)]
                    if slab_i >= 2:
                        w.append((s_pe, tr_marks[slab_i - 2]))
                    b.op("vector", "tensor_copy",
                         out=slab_b[:, par * HID: par * HID + cols],
                         in_=slab_f[:, par * HID: par * HID + cols],
                         waits=w, inc=s_dve)
                    conv_marks[slab_i] = b.c(s_dve)
                    for cc in range(ncc):
                        pp = tcount % 4
                        w = [(s_dve, conv_marks[slab_i])]
                        if tcount >= 4:
                            w.append(evac_marks[tcount - 4])
                        if cc == 0 and slab_i >= 2:
                            w.append(st_marks[slab_i - 2])
                        b.op("tensor", "transpose", ps_prep[pp][:],
                             slab_b[:, par * HID + cc * 128: par * HID + (cc + 1) * 128],
                             ident_b[:], waits=w, inc=s_pe)
                        pev = b.c(s_pe)
                        w = [(s_pe, pev)]
                        if cc == 0 and slab_i >= 2:
                            w.append(st_marks[slab_i - 2])
                        if tcount % 2 == 0:
                            b.op("vector", "tensor_copy",
                                 out=tout[:, par * HID + cc * 128: par * HID + (cc + 1) * 128],
                                 in_=ps_prep[pp][:], waits=w, inc=s_dve)
                            evac_marks[tcount] = (s_dve, b.c(s_dve))
                        else:
                            b.op("scalar", "copy",
                                 out=tout[:, par * HID + cc * 128: par * HID + (cc + 1) * 128],
                                 in_=ps_prep[pp][:], waits=w, inc=s_act)
                            evac_marks[tcount] = (s_act, b.c(s_act))
                        tcount += 1
                    stsem = STS[slab_i % 2]
                    b.op("sync", "dma_start",
                         out=dst.ap()[rc * 128:(rc + 1) * 128, :],
                         in_=tout[:, par * HID: par * HID + cols],
                         waits=[evac_marks[tcount - 1], evac_marks[tcount - 2]],
                         inc=stsem, incn=16)
                    st_marks[slab_i] = (stsem, b.c(stsem))
                    tr_marks[slab_i] = b.c(s_pe)
                    slab_i += 1
        b.op("sync", None, waits=st_all())
        dump_consts(0)

        pst_hist = []

        def emit_ln(u, from_fm, ps_tr_f, ps_tr_b, xload_marks=None,
                    src_marks=None):
            ln_ev = {}
            if from_fm:
                for g in range(4):
                    for dc in range(DC):
                        k = g * DC + dc
                        w = list(src_marks) if k == 0 else []
                        if k >= 2:
                            w.append(ln_ev[k - 2])
                        b.op("tensor", "transpose", ps_tr_f[k % 2][:],
                             u.x_f[:, dc * TILE + g * 128: dc * TILE + (g + 1) * 128],
                             ident_f[:], waits=w, inc=s_pe)
                        if k % 2 == 0:
                            b.op("vector", "tensor_copy",
                                 out=u.x_tm[:, g * D + dc * 128: g * D + (dc + 1) * 128],
                                 in_=ps_tr_f[0][:], waits=[(s_pe, b.c(s_pe))],
                                 inc=s_dve)
                            ln_ev[k] = (s_dve, b.c(s_dve))
                        else:
                            b.op("scalar", "copy",
                                 out=u.x_tm[:, g * D + dc * 128: g * D + (dc + 1) * 128],
                                 in_=ps_tr_f[1][:], waits=[(s_pe, b.c(s_pe))],
                                 inc=s_act)
                            ln_ev[k] = (s_act, b.c(s_act))
                tm_ready = [(s_dve, b.c(s_dve)), (s_act, b.c(s_act))]
                grp_ready = [tm_ready for _ in range(4)]
            else:
                grp_ready = [[xload_marks[g]] for g in range(4)]

            for g in range(4):
                b.op("scalar", "activation", out=u.sq_scr[:],
                     in_=u.x_tm[:, g * D:(g + 1) * D], func=AF.Square,
                     accum_out=ssq4[:, g:g + 1], waits=grp_ready[g], inc=s_act)
                b.op("vector", "reduce_sum", out=rs4[:, g:g + 1],
                     in_=u.x_tm[:, g * D:(g + 1) * D], axis=AXX,
                     waits=grp_ready[g], inc=s_dve)
            sq_done = b.c(s_act)
            b.op("vector", "tensor_scalar_mul", mean4[:], rs4[:], 1.0 / D,
                 waits=[(s_dve, b.c(s_dve))], inc=s_dve)
            b.op("vector", "tensor_scalar_mul", esq4[:], ssq4[:], 1.0 / D,
                 waits=[(s_act, sq_done)], inc=s_dve)
            b.op("vector", "tensor_mul", m24[:], mean4[:], mean4[:],
                 waits=[(s_dve, b.c(s_dve))], inc=s_dve)
            b.op("vector", "tensor_sub", var4[:], esq4[:], m24[:],
                 waits=[(s_dve, b.c(s_dve))], inc=s_dve)
            b.op("vector", "tensor_scalar_add", var4[:], var4[:], EPS,
                 waits=[(s_dve, b.c(s_dve))], inc=s_dve)
            b.op("vector", "reciprocal", rvar4[:], var4[:],
                 waits=[(s_dve, b.c(s_dve))], inc=s_dve)
            vd = b.c(s_dve)
            b.op("scalar", "activation", out=rstd4[:], in_=rvar4[:],
                 func=AF.Sqrt, waits=[(s_dve, vd)], inc=s_act)
            rstd_done = b.c(s_act)
            nrm_ev = {}
            for g in range(4):
                b.op("vector", "tensor_scalar", out=u.xn_tm[:],
                     in0=u.x_tm[:, g * D:(g + 1) * D],
                     scalar1=mean4[:, g:g + 1], scalar2=rstd4[:, g:g + 1],
                     op0=ALU.subtract, op1=ALU.mult,
                     waits=[(s_act, rstd_done)]
                     + ([nrm_ev[(g - 1) * DC + DC - 1]] if g > 0 else []),
                     inc=s_dve)
                nz = b.c(s_dve)
                for dc in range(DC):
                    k = g * DC + dc
                    w = [(s_dve, nz)]
                    if k >= 2:
                        w.append(nrm_ev[k - 2])
                    b.op("tensor", "transpose", ps_tr_b[k % 2][:],
                         u.xn_tm[:, dc * 128:(dc + 1) * 128], ident_b[:],
                         waits=w, inc=s_pe)
                    if k % 2 == 0:
                        b.op("vector", "tensor_copy",
                             out=u.xn_f[:, dc * TILE + g * 128: dc * TILE + (g + 1) * 128],
                             in_=ps_tr_b[0][:], waits=[(s_pe, b.c(s_pe))],
                             inc=s_dve)
                        nrm_ev[k] = (s_dve, b.c(s_dve))
                    else:
                        b.op("scalar", "copy",
                             out=u.xn_f[:, dc * TILE + g * 128: dc * TILE + (g + 1) * 128],
                             in_=ps_tr_b[1][:], waits=[(s_pe, b.c(s_pe))],
                             inc=s_act)
                        nrm_ev[k] = (s_act, b.c(s_act))
            return [(s_dve, b.c(s_dve)), (s_act, b.c(s_act))]

        def emit_rope_tables(u, t):
            for si in range(4):
                slot = t * 4 + si
                cs = slice(si * 128, (si + 1) * 128)
                b.op("vector", "tensor_scalar", out=u.rsc1[:, cs], in0=jtile[:],
                     scalar1=freq[:], scalar2=qf_all[:, slot:slot + 1],
                     op0=ALU.mult, op1=ALU.add,
                     inc=(s_dve if si == 3 else None))
            b.op("vector", "engine_nop", waits=[(s_dve, b.c(s_dve))])
            for (dst, shift) in ((u.rope_sp, 0.0), (u.rope_c, float(np.pi / 2))):
                if shift:
                    b.op("vector", "tensor_scalar_add", u.rsc1[:], u.rsc1[:],
                         shift, inc=s_dve)
                    b.op("vector", "engine_nop", waits=[(s_dve, b.c(s_dve))])
                b.op("vector", "tensor_scalar", out=u.rsc2[:], in0=u.rsc1[:],
                     scalar1=INV2PI, scalar2=RMAGIC, op0=ALU.mult, op1=ALU.add,
                     inc=s_dve)
                b.op("vector", "tensor_scalar_sub", u.rsc2[:], u.rsc2[:],
                     RMAGIC, waits=[(s_dve, b.c(s_dve))], inc=s_dve)
                b.op("vector", "scalar_tensor_tensor", out=dst[:], in0=u.rsc2[:],
                     scalar=-TWO_PI_HI, in1=u.rsc1[:], op0=ALU.mult,
                     op1=ALU.add, waits=[(s_dve, b.c(s_dve))], inc=s_dve)
                b.op("vector", "scalar_tensor_tensor", out=dst[:], in0=u.rsc2[:],
                     scalar=-TWO_PI_LO, in1=dst[:], op0=ALU.mult, op1=ALU.add,
                     waits=[(s_dve, b.c(s_dve))], inc=s_dve)
                b.op("vector", "tensor_scalar", out=dst[:], in0=dst[:],
                     scalar1=float(-np.pi), scalar2=float(np.pi),
                     op0=ALU.max, op1=ALU.min,
                     waits=[(s_dve, b.c(s_dve))], inc=s_dve)
                sv = b.c(s_dve)
                b.op("scalar", "activation", out=dst[:], in_=dst[:], func=AF.Sin,
                     waits=[(s_dve, sv)], inc=s_act)
            sa = b.c(s_act)
            b.op("vector", "tensor_scalar", out=u.rope_sp[:], in0=u.rope_sp[:],
                 scalar1=altsign[:], scalar2=None, op0=ALU.mult,
                 waits=[(s_act, sa)], inc=s_dve)
            return [(s_act, b.c(s_act)), (s_dve, b.c(s_dve))]

        def emit_proj(u, t, wsrc, nout, dst, rope_pred, ps_pj, xn_marks,
                      rope_marks):
            tc = slice(t * TILE, (t + 1) * TILE)
            pe0 = b.c(s_pe)
            pj_pe, pj_ev = {}, {}
            for oc in range(nout):
                par = oc % 2
                w = [(s_pe, pj_pe[oc - 2])] if oc >= 2 else [(s_pe, pe0)]
                b.op("sync", "dma_start",
                     out=u.wbig[:, par * HID: par * HID + D],
                     in_=wsrc.ap()[oc * 128:(oc + 1) * 128, :],
                     waits=w, inc=WPAR[par], incn=16)
                wl = b.c(WPAR[par])
                w = [(WPAR[par], wl)]
                if oc == 0:
                    w += xn_marks + rope_marks + adw()
                if oc >= 2:
                    w.append(pj_ev[oc - 2])
                for dc in range(DC):
                    b.op("tensor", "matmul", ps_pj[par][:],
                         u.wbig[:, par * HID + dc * 128: par * HID + (dc + 1) * 128],
                         u.xn_f[:, dc * TILE:(dc + 1) * TILE],
                         start=(dc == 0), stop=(dc == DC - 1),
                         waits=(w if dc == 0 else []),
                         inc=(s_pe if dc == DC - 1 else None))
                pj_pe[oc] = b.c(s_pe)
                idx = len(pst_hist)
                sg = idx % 4
                ssl = slice(sg * TILE, (sg + 1) * TILE)
                w = [(s_pe, pj_pe[oc])]
                if idx >= 4:
                    w.append(pst_hist[idx - 4])
                if rope_pred(oc):
                    b.op("vector", "tensor_copy",
                         out=u.silu_sb[:, par * TILE:(par + 1) * TILE],
                         in_=ps_pj[par][:], waits=w, inc=s_dve)
                    b.op("vector", "stream_shuffle", u.rsc1[:],
                         u.silu_sb[:, par * TILE:(par + 1) * TILE], SWAP32)
                    b.op("vector", "tensor_mul", u.rsc2[:],
                         u.silu_sb[:, par * TILE:(par + 1) * TILE], u.rope_c[:])
                    b.op("vector", "tensor_mul", u.rsc1[:], u.rsc1[:],
                         u.rope_sp[:])
                    b.op("vector", "tensor_add", u.pstage[:, ssl], u.rsc2[:],
                         u.rsc1[:], inc=s_dve)
                else:
                    b.op("vector", "tensor_copy", out=u.pstage[:, ssl],
                         in_=ps_pj[par][:], waits=w, inc=s_dve)
                pj_ev[oc] = (s_dve, b.c(s_dve))
                b.op("sync", "dma_start",
                     out=dst.ap()[oc * 128:(oc + 1) * 128, tc],
                     in_=u.pstage[:, ssl], waits=[pj_ev[oc]], inc=STS[2],
                     incn=16)
                pst_hist.append((STS[2], b.c(STS[2])))

        def emit_ffn(u, t, src_kind, src_dram, ps_tr_f, ps_tr_b, ps_ab, ps_dn):
            tc = slice(t * TILE, (t + 1) * TILE)
            if src_kind == "tm":
                xl = []
                for g in range(4):
                    r0 = t * TILE + g * 128
                    w = adw() + st_all() if g == 0 else []
                    b.op("sync", "dma_start", out=u.x_tm[:, g * D:(g + 1) * D],
                         in_=src_dram.ap()[r0:r0 + 128, :],
                         waits=w, inc=s_ldt, incn=16)
                    xl.append((s_ldt, b.c(s_ldt)))
                ln_marks = emit_ln(u, False, ps_tr_f, ps_tr_b, xload_marks=xl)
                xf_ev = {}
                for g in range(4):
                    for dc in range(DC):
                        k = g * DC + dc
                        w = [xl[g]]
                        if k >= 2:
                            w.append(xf_ev[k - 2])
                        b.op("tensor", "transpose", ps_tr_f[k % 2][:],
                             u.x_tm[:, g * D + dc * 128: g * D + (dc + 1) * 128],
                             ident_f[:], waits=w, inc=s_pe)
                        if k % 2 == 0:
                            b.op("vector", "tensor_copy",
                                 out=u.x_f[:, dc * TILE + g * 128: dc * TILE + (g + 1) * 128],
                                 in_=ps_tr_f[0][:], waits=[(s_pe, b.c(s_pe))],
                                 inc=s_dve)
                            xf_ev[k] = (s_dve, b.c(s_dve))
                        else:
                            b.op("scalar", "copy",
                                 out=u.x_f[:, dc * TILE + g * 128: dc * TILE + (g + 1) * 128],
                                 in_=ps_tr_f[1][:], waits=[(s_pe, b.c(s_pe))],
                                 inc=s_act)
                            xf_ev[k] = (s_act, b.c(s_act))
            else:
                for dc in range(DC):
                    w = adw() + st_all() if dc == 0 else []
                    b.op("sync", "dma_start",
                         out=u.x_f[:, dc * TILE:(dc + 1) * TILE],
                         in_=src_dram.ap()[dc * 128:(dc + 1) * 128, tc],
                         waits=w, inc=s_ldt, incn=16)
                xl0 = (s_ldt, b.c(s_ldt))
                ln_marks = emit_ln(u, True, ps_tr_f, ps_tr_b, src_marks=[xl0])

            pe0 = b.c(s_pe)
            f1_pe, f1_sil, f1_mul = {}, {}, {}
            for hc in range(HC):
                par = hc % 2
                w = [(s_pe, f1_pe[hc - 2])] if hc >= 2 else [(s_pe, pe0)]
                b.op("sync", "dma_start",
                     out=u.wbig[:, par * HID: par * HID + D],
                     in_=wg_t.ap()[hc * 128:(hc + 1) * 128, :],
                     waits=w, inc=WPAR[par], incn=16)
                b.op("sync", "dma_start",
                     out=u.wbig[:, par * HID + D: par * HID + 2 * D],
                     in_=wu_t.ap()[hc * 128:(hc + 1) * 128, :],
                     inc=WPAR[par], incn=16)
                wl = b.c(WPAR[par])
                w = [(WPAR[par], wl)]
                if hc == 0:
                    w += ln_marks + adw()
                if hc >= 2:
                    w.append((s_act, f1_sil[hc - 2]))
                    w.append((s_dve, f1_mul[hc - 2]))
                for dc in range(DC):
                    b.op("tensor", "matmul", ps_ab[par][:],
                         u.wbig[:, par * HID + dc * 128: par * HID + (dc + 1) * 128],
                         u.xn_f[:, dc * TILE:(dc + 1) * TILE],
                         start=(dc == 0), stop=(dc == DC - 1),
                         waits=(w if dc == 0 else []),
                         inc=(s_pe if dc == DC - 1 else None))
                geval = b.c(s_pe)
                for dc in range(DC):
                    b.op("tensor", "matmul", ps_ab[2 + par][:],
                         u.wbig[:, par * HID + D + dc * 128: par * HID + D + (dc + 1) * 128],
                         u.xn_f[:, dc * TILE:(dc + 1) * TILE],
                         start=(dc == 0), stop=(dc == DC - 1),
                         inc=(s_pe if dc == DC - 1 else None))
                f1_pe[hc] = b.c(s_pe)
                b.op("scalar", "activation",
                     out=u.silu_sb[:, par * TILE:(par + 1) * TILE],
                     in_=ps_ab[par][:], func=AF.Silu,
                     waits=[(s_pe, geval)], inc=s_act)
                f1_sil[hc] = b.c(s_act)
                b.op("vector", "tensor_mul",
                     u.gated[:, hc * TILE:(hc + 1) * TILE],
                     u.silu_sb[:, par * TILE:(par + 1) * TILE],
                     ps_ab[2 + par][:],
                     waits=[(s_act, f1_sil[hc]), (s_pe, f1_pe[hc])], inc=s_dve)
                f1_mul[hc] = b.c(s_dve)

            pe0 = b.c(s_pe)
            f2_pe, f2_ev = {}, {}
            for oc in range(DC):
                par = oc % 2
                w = [(s_pe, f2_pe[oc - 2])] if oc >= 2 else [(s_pe, pe0)]
                b.op("sync", "dma_start", out=u.wbig[:, par * HID:(par + 1) * HID],
                     in_=wd_t.ap()[oc * 128:(oc + 1) * 128, :],
                     waits=w, inc=WPAR[par], incn=16)
                wl = b.c(WPAR[par])
                w = [(WPAR[par], wl)]
                if oc == 0:
                    w += [(s_dve, b.c(s_dve))] + adw()
                if oc >= 2:
                    w.append(f2_ev[oc - 2])
                for hc in range(HC):
                    b.op("tensor", "matmul", ps_dn[par][:],
                         u.wbig[:, par * HID + hc * 128: par * HID + (hc + 1) * 128],
                         u.gated[:, hc * TILE:(hc + 1) * TILE],
                         start=(hc == 0), stop=(hc == HC - 1),
                         waits=(w if hc == 0 else []),
                         inc=(s_pe if hc == HC - 1 else None))
                f2_pe[oc] = b.c(s_pe)
                b.op("vector", "scalar_tensor_tensor",
                     out=u.x_f[:, oc * TILE:(oc + 1) * TILE],
                     in0=ps_dn[par][:], scalar=0.5,
                     in1=u.x_f[:, oc * TILE:(oc + 1) * TILE],
                     op0=ALU.mult, op1=ALU.add,
                     waits=[(s_pe, f2_pe[oc])], inc=s_dve)
                f2_ev[oc] = (s_dve, b.c(s_dve))
            return f2_ev

        if limit >= 1:
         with ExitStack() as p1:
            u = NS()
            u.x_tm = p1.enter_context(nc.sbuf_tensor("p1_xtm", [128, 4 * D], F32))
            u.xn_tm = p1.enter_context(nc.sbuf_tensor("p1_xntm", [128, D], BF16))
            u.x_f = p1.enter_context(nc.sbuf_tensor("p1_xf", [128, DC * TILE], F32))
            u.xn_f = p1.enter_context(nc.sbuf_tensor("p1_xnf", [128, DC * TILE], BF16))
            u.sq_scr = p1.enter_context(nc.sbuf_tensor("p1_sq", [128, D], BF16))
            u.gated = p1.enter_context(nc.sbuf_tensor("p1_gated", [128, HC * TILE], BF16))
            u.wbig = p1.enter_context(nc.sbuf_tensor("p1_wbig", [128, 2 * HID], BF16))
            u.silu_sb = p1.enter_context(nc.sbuf_tensor("p1_silu", [128, 2 * TILE], F32))
            u.pstage = p1.enter_context(nc.sbuf_tensor("p1_pst", [128, 4 * TILE], BF16))
            u.rope_c = p1.enter_context(nc.sbuf_tensor("p1_rc", [128, TILE], F32))
            u.rope_sp = p1.enter_context(nc.sbuf_tensor("p1_rsp", [128, TILE], F32))
            u.rsc1 = p1.enter_context(nc.sbuf_tensor("p1_rs1", [128, TILE], F32))
            u.rsc2 = p1.enter_context(nc.sbuf_tensor("p1_rs2", [128, TILE], F32))
            ps_tr_f = [p1.enter_context(nc.psum_tensor(f"p1_trf{i}", [128, 128], F32))
                       for i in range(2)]
            ps_tr_b = [p1.enter_context(nc.psum_tensor(f"p1_trb{i}", [128, 128], BF16))
                       for i in range(2)]
            ps_ab = [p1.enter_context(nc.psum_tensor(f"p1_ab{i}", [128, TILE], F32))
                     for i in range(4)]
            ps_dn = [ps_ab[0], ps_ab[1]]

            for t in range(NTILE):
                tc = slice(t * TILE, (t + 1) * TILE)
                res_marks = emit_ffn(u, t, "tm", x_d, ps_tr_f, ps_tr_b,
                                     ps_ab, ps_dn)
                if debug and t == 0:
                    dump_consts(2)
                for dc in range(DC):
                    b.op("sync", "dma_start",
                         out=x1_d.ap()[dc * 128:(dc + 1) * 128, tc],
                         in_=u.x_f[:, dc * TILE:(dc + 1) * TILE],
                         waits=[res_marks[dc]], inc=STS[2], incn=16)
                xn_marks = emit_ln(u, True, ps_tr_f, ps_tr_b,
                                   src_marks=[res_marks[DC - 1],
                                              res_marks[DC - 2]] + adw())
                if debug and t == 0:
                    dump_consts(3)
                rope_marks = emit_rope_tables(u, t)
                if debug:
                    b.op("sync", "dma_start",
                         out=dbg["ropec"].ap()[:, t * TILE:(t + 1) * TILE],
                         in_=u.rope_c[:], waits=rope_marks, inc=STS[5], incn=16)
                    b.op("sync", "dma_start",
                         out=dbg["ropesp"].ap()[:, t * TILE:(t + 1) * TILE],
                         in_=u.rope_sp[:], waits=rope_marks, inc=STS[5], incn=16)
                emit_proj(u, t, kvw_t, 8, kvag_in, lambda oc: oc < 4,
                          [ps_ab[2], ps_ab[3]], xn_marks, rope_marks)
                if debug and t == 0:
                    dump_consts(4)
                emit_proj(u, t, qw_t, DC, q_d, lambda oc: True,
                          [ps_ab[2], ps_ab[3]], xn_marks, rope_marks)
                if debug and t == 0:
                    dump_consts(5)
                if debug and t == 1:
                    dump_consts(6)

        if limit >= 2:
          b.op("gpsimd", "collective_compute", "AllGather", ALU.bypass,
             replica_groups=(cc_groups or [[0, 1], [2, 3], [4, 5], [6, 7]]),
             ins=[kvag_in.ap()], outs=[kvag_out.ap()],
             waits=st_all(), inc=s_cc)

        def owner(g):
            if g <= 15:
                return (g % 2, g if g % 2 == 0 else g - 1)
            return (1 - g % 2, 31 - g if g % 2 == 0 else 32 - g)

        if limit >= 3:
         with ExitStack() as p4:
            a = NS()
            a.kf = p4.enter_context(nc.sbuf_tensor("a_kf", [128, HKV * T], BF16))
            a.vt = p4.enter_context(nc.sbuf_tensor("a_vt", [128, HKV * T], BF16))
            a.qg = p4.enter_context(nc.sbuf_tensor("a_qg", [128, HQ * TILE], BF16))
            a.attn = p4.enter_context(nc.sbuf_tensor("a_attn", [128, HQ * TILE], BF16))
            a.srow = p4.enter_context(nc.sbuf_tensor("a_srow", [128, T], F32))
            a.p4 = p4.enter_context(nc.sbuf_tensor("a_p4", [128, 4 * T], BF16))
            a.ptsb = p4.enter_context(nc.sbuf_tensor("a_ptsb", [128, 2 * 512], BF16))
            a.masks = p4.enter_context(nc.sbuf_tensor("a_masks", [128, 4 * KB], F32))
            a.resld = p4.enter_context(nc.sbuf_tensor("a_res", [128, 2 * TILE], F32))
            a.x2buf = p4.enter_context(nc.sbuf_tensor("a_x2", [128, 2 * TILE], F32))
            a.cwb = p4.enter_context(nc.sbuf_tensor("a_cwb", [128, 2 * D], BF16))
            a.vstage = p4.enter_context(nc.sbuf_tensor("a_vst", [128, 2 * 128], BF16))

            with ExitStack() as p3:
                ps_vtr = [p3.enter_context(nc.psum_tensor(f"a_vtr{i}", [128, 128], BF16))
                          for i in range(2)]
                v_pe, v_ev = {}, {}
                for h in range(HKV):
                    for g in range(32):
                        r, s = owner(g)
                        b.op("sync", "dma_start",
                             out=a.kf[:, h * T + g * 128: h * T + (g + 1) * 128],
                             in_=kvag_out.ap()[r * 1024 + h * 128:
                                               r * 1024 + (h + 1) * 128,
                                               s * 128:(s + 1) * 128],
                             waits=[(s_cc, 1)], inc=s_ldt, incn=16)
                        k = h * 32 + g
                        w = [(s_cc, 1)]
                        if k >= 2:
                            w.append((s_pe, v_pe[k - 2]))
                        b.op("sync", "dma_start",
                             out=a.vstage[:, (k % 2) * 128:(k % 2 + 1) * 128],
                             in_=kvag_out.ap()[r * 1024 + 512 + h * 128:
                                               r * 1024 + 512 + (h + 1) * 128,
                                               s * 128:(s + 1) * 128],
                             waits=w, inc=VPAR[k % 2], incn=16)
                        w = [(VPAR[k % 2], b.c(VPAR[k % 2]))]
                        if k == 0:
                            w += adw()
                        if k >= 2:
                            w.append(v_ev[k - 2])
                        b.op("tensor", "transpose", ps_vtr[k % 2][:],
                             a.vstage[:, (k % 2) * 128:(k % 2 + 1) * 128],
                             ident_b[:], waits=w, inc=s_pe)
                        v_pe[k] = b.c(s_pe)
                        if k % 2 == 0:
                            b.op("vector", "tensor_copy",
                                 out=a.vt[:, k * 128:(k + 1) * 128],
                                 in_=ps_vtr[0][:], waits=[(s_pe, v_pe[k])],
                                 inc=s_dve)
                            v_ev[k] = (s_dve, b.c(s_dve))
                        else:
                            b.op("scalar", "copy",
                                 out=a.vt[:, k * 128:(k + 1) * 128],
                                 in_=ps_vtr[1][:], waits=[(s_pe, v_pe[k])],
                                 inc=s_act)
                            v_ev[k] = (s_act, b.c(s_act))
                kv_ready = [(s_ldt, b.c(s_ldt)), (s_dve, b.c(s_dve)),
                            (s_act, b.c(s_act))]

            for t in (range(NTILE) if limit >= 4 else []):
                tc = slice(t * TILE, (t + 1) * TILE)
                for hq in range(HQ):
                    w = kv_ready + [(s_pe, b.c(s_pe))] if hq == 0 else []
                    b.op("sync", "dma_start",
                         out=a.qg[:, hq * TILE:(hq + 1) * TILE],
                         in_=q_d.ap()[hq * 128:(hq + 1) * 128, tc],
                         waits=w, inc=s_ldt, incn=16)
                q_ready = (s_ldt, b.c(s_ldt))
                for si in range(4):
                    slot = t * 4 + si
                    ms = slice(si * KB, (si + 1) * KB)
                    b.op("vector", "tensor_scalar", out=a.masks[:, ms],
                         in0=dtile[:], scalar1=qpos_cols[:, slot:slot + 1],
                         scalar2=float(512.0 * (EXT[slot] - 1)),
                         op0=ALU.subtract, op1=ALU.add,
                         waits=(adw() if si == 0 else []))
                    b.op("vector", "tensor_single_scalar", out=a.masks[:, ms],
                         in_=a.masks[:, ms], scalar=0.0, op=ALU.is_gt)
                    b.op("vector", "tensor_scalar_mul", a.masks[:, ms],
                         a.masks[:, ms], MASKVAL, inc=s_dve)
                mask_ready = (s_dve, b.c(s_dve))

                with ExitStack() as pi:
                    if alimit < 1:
                        continue
                    ps_s = [pi.enter_context(nc.psum_tensor(f"a_s{t}{i}", [128, KB], F32))
                            for i in range(2)]
                    ps_pt = [pi.enter_context(nc.psum_tensor(f"a_pt{t}{i}", [128, 128], BF16))
                             for i in range(4)]
                    ps_pv = pi.enter_context(nc.psum_tensor(f"a_pv{t}", [128, 512], F32))
                    sbank = [None, None]
                    ptlast = None
                    pvlast = None
                    first = True
                    for si in range(4):
                        slot = t * 4 + si
                        e = EXT[slot]
                        nch = 4 * e
                        for kvh in range(HKV):
                            for qh in range(4):
                                hq = kvh * 4 + qh
                                for blk in range(e):
                                    bk = blk % 2
                                    w = []
                                    if first:
                                        w += [q_ready, mask_ready] + adw()
                                        first = False
                                    if sbank[bk] is not None:
                                        w.append(sbank[bk])
                                    b.op("tensor", "matmul", ps_s[bk][:],
                                         a.qg[:, hq * TILE + si * 128:
                                              hq * TILE + (si + 1) * 128],
                                         a.kf[:, kvh * T + blk * KB:
                                              kvh * T + (blk + 1) * KB],
                                         start=True, stop=True, waits=w,
                                         inc=s_pe)
                                    pm = b.c(s_pe)
                                    if blk == e - 1:
                                        b.op("vector", "tensor_tensor",
                                             out=a.srow[:, blk * KB:(blk + 1) * KB],
                                             in0=ps_s[bk][:],
                                             in1=a.masks[:, si * KB:(si + 1) * KB],
                                             op=ALU.add,
                                             waits=[(s_pe, pm),
                                                    (s_act, b.c(s_act))],
                                             inc=s_dve)
                                        sbank[bk] = (s_dve, b.c(s_dve))
                                    else:
                                        b.op("scalar", "copy",
                                             out=a.srow[:, blk * KB:(blk + 1) * KB],
                                             in_=ps_s[bk][:],
                                             waits=[(s_pe, pm)], inc=s_act)
                                        sbank[bk] = (s_act, b.c(s_act))
                                evw = [sbank[0], sbank[1],
                                       (s_pe, b.c(s_pe))]
                                evw = [x for x in evw if x is not None]
                                b.op("scalar", "activation",
                                     out=a.p4[:, qh * T: qh * T + e * KB],
                                     in_=a.srow[:, 0:e * KB], func=AF.Exp,
                                     bias=negB[:], scale=SCALE,
                                     accum_out=sums4[:, qh:qh + 1],
                                     waits=evw, inc=s_act)
                            exp_done = b.c(s_act)
                            if alimit < 2:
                                continue
                            b.op("vector", "reciprocal", recip4[:], sums4[:],
                                 waits=[(s_act, exp_done)], inc=s_dve)
                            rc_done = b.c(s_dve)
                            for qh in range(4):
                                b.op("vector", "tensor_scalar",
                                     out=a.p4[:, qh * T: qh * T + e * KB],
                                     in0=a.p4[:, qh * T: qh * T + e * KB],
                                     scalar1=recip4[:, qh:qh + 1], scalar2=None,
                                     op0=ALU.mult, waits=[(s_dve, rc_done)],
                                     inc=s_dve)
                            pnorm = b.c(s_dve)
                            if alimit < 3:
                                continue
                            for c in range(nch):
                                w = [(s_dve, pnorm)] if c == 0 else []
                                if c == 0 and pvlast is not None:
                                    w += pvlast
                                if ptlast is not None:
                                    w += ptlast
                                for qh in range(4):
                                    b.op("tensor", "transpose", ps_pt[qh][:],
                                         a.p4[:, qh * T + c * 128:
                                              qh * T + (c + 1) * 128],
                                         ident_b[:],
                                         waits=(w if qh == 0 else []),
                                         inc=(s_pe if qh == 3 else None))
                                trm = b.c(s_pe)
                                cs = (c % 2) * 512
                                for qh in range(4):
                                    if qh % 2 == 0:
                                        b.op("vector", "tensor_copy",
                                             out=a.ptsb[:, cs + qh * 128:
                                                        cs + (qh + 1) * 128],
                                             in_=ps_pt[qh][:],
                                             waits=[(s_pe, trm)], inc=s_dve)
                                    else:
                                        b.op("scalar", "copy",
                                             out=a.ptsb[:, cs + qh * 128:
                                                        cs + (qh + 1) * 128],
                                             in_=ps_pt[qh][:],
                                             waits=[(s_pe, trm)], inc=s_act)
                                ptlast = [(s_dve, b.c(s_dve)),
                                          (s_act, b.c(s_act))]
                                b.op("tensor", "matmul", ps_pv[:],
                                     a.vt[:, (kvh * 32 + c) * 128:
                                          (kvh * 32 + c + 1) * 128],
                                     a.ptsb[:, cs:cs + 512],
                                     start=(c == 0), stop=(c == nch - 1),
                                     waits=ptlast,
                                     inc=(s_pe if c == nch - 1 else None))
                            pv_done = b.c(s_pe)
                            oc0 = (4 * kvh) * TILE + si * 128
                            b.op("vector", "tensor_copy",
                                 out=a.attn[:, :].rearrange(
                                     "p (h u) -> p h u", h=HQ)[
                                     :, 4 * kvh:4 * kvh + 4,
                                     si * 128:(si + 1) * 128],
                                 in_=ps_pv[:, :].rearrange(
                                     "p (h u) -> p h u", h=4),
                                 waits=[(s_pe, pv_done)], inc=s_dve)
                            pvlast = [(s_dve, b.c(s_dve))]
                    attn_all = [(s_dve, b.c(s_dve)), (s_act, b.c(s_act))]

                if alimit < 4:
                    continue
                with ExitStack() as pc:
                    ps_c = [pc.enter_context(nc.psum_tensor(f"a_c{t}{i}", [128, TILE], F32))
                            for i in range(2)]
                    pe0 = b.c(s_pe)
                    cp_pe, cp_ev, cp_st = {}, {}, {}
                    for oc in range(DC):
                        par = oc % 2
                        w = [(s_pe, cp_pe[oc - 2])] if oc >= 2 else [(s_pe, pe0)]
                        b.op("sync", "dma_start",
                             out=a.cwb[:, par * D:(par + 1) * D],
                             in_=cw_t.ap()[oc * 128:(oc + 1) * 128, :],
                             waits=w, inc=WPAR[par], incn=16)
                        wl = b.c(WPAR[par])
                        w2 = [cp_ev[oc - 2]] if oc >= 2 else []
                        b.op("sync", "dma_start",
                             out=a.resld[:, par * TILE:(par + 1) * TILE],
                             in_=x1_d.ap()[oc * 128:(oc + 1) * 128, tc],
                             waits=w2, inc=VPAR[par], incn=16)
                        rl = b.c(VPAR[par])
                        w = [(WPAR[par], wl)]
                        if oc == 0:
                            w += attn_all + adw()
                        if oc >= 2:
                            w.append(cp_ev[oc - 2])
                        for dc in range(DC):
                            b.op("tensor", "matmul", ps_c[par][:],
                                 a.cwb[:, par * D + dc * 128: par * D + (dc + 1) * 128],
                                 a.attn[:, dc * TILE:(dc + 1) * TILE],
                                 start=(dc == 0), stop=(dc == DC - 1),
                                 waits=(w if dc == 0 else []),
                                 inc=(s_pe if dc == DC - 1 else None))
                        cp_pe[oc] = b.c(s_pe)
                        w = [(s_pe, cp_pe[oc]), (VPAR[par], rl)]
                        if oc >= 2:
                            w.append(cp_st[oc - 2])
                        b.op("vector", "tensor_add",
                             a.x2buf[:, par * TILE:(par + 1) * TILE],
                             ps_c[par][:], a.resld[:, par * TILE:(par + 1) * TILE],
                             waits=w, inc=s_dve)
                        cp_ev[oc] = (s_dve, b.c(s_dve))
                        b.op("sync", "dma_start",
                             out=x2_d.ap()[oc * 128:(oc + 1) * 128, tc],
                             in_=a.x2buf[:, par * TILE:(par + 1) * TILE],
                             waits=[cp_ev[oc]], inc=STS[3], incn=16)
                        cp_st[oc] = (STS[3], b.c(STS[3]))

        if limit >= 5:
         with ExitStack() as p5:
            u = NS()
            u.x_tm = p5.enter_context(nc.sbuf_tensor("p5_xtm", [128, 4 * D], F32))
            u.xn_tm = p5.enter_context(nc.sbuf_tensor("p5_xntm", [128, D], BF16))
            u.x_f = p5.enter_context(nc.sbuf_tensor("p5_xf", [128, DC * TILE], F32))
            u.xn_f = p5.enter_context(nc.sbuf_tensor("p5_xnf", [128, DC * TILE], BF16))
            u.sq_scr = p5.enter_context(nc.sbuf_tensor("p5_sq", [128, D], BF16))
            u.gated = p5.enter_context(nc.sbuf_tensor("p5_gated", [128, HC * TILE], BF16))
            u.wbig = p5.enter_context(nc.sbuf_tensor("p5_wbig", [128, 2 * HID], BF16))
            u.silu_sb = p5.enter_context(nc.sbuf_tensor("p5_silu", [128, 2 * TILE], F32))
            u.stage = p5.enter_context(nc.sbuf_tensor("p5_stage", [128, 8 * 128], F32))
            ps_tr_f = [p5.enter_context(nc.psum_tensor(f"p5_trf{i}", [128, 128], F32))
                       for i in range(2)]
            ps_tr_b = [p5.enter_context(nc.psum_tensor(f"p5_trb{i}", [128, 128], BF16))
                       for i in range(2)]
            ps_ab = [p5.enter_context(nc.psum_tensor(f"p5_ab{i}", [128, TILE], F32))
                     for i in range(4)]
            ps_dn = [ps_ab[0], ps_ab[1]]

            for t in range(NTILE):
                res_marks = emit_ffn(u, t, "fm", x2_d, ps_tr_f, ps_tr_b,
                                     ps_ab, ps_dn)
                if debug:
                    if t == 0:
                        b.op("sync", "dma_start", out=dbg["xn5"].ap(),
                             in_=u.xn_f[:], waits=[(s_dve, b.c(s_dve)),
                                                   (s_act, b.c(s_act))],
                             inc=STS[5], incn=16)
                    for dc in range(DC):
                        b.op("sync", "dma_start",
                             out=dbg["x3"].ap()[dc * 128:(dc + 1) * 128,
                                                t * TILE:(t + 1) * TILE],
                             in_=u.x_f[:, dc * TILE:(dc + 1) * TILE],
                             waits=[res_marks[dc]], inc=STS[5], incn=16)
                o_ev, o_st = {}, {}
                for dc in range(DC):
                    for g in range(4):
                        k = dc * 4 + g
                        w = [res_marks[dc]]
                        if k >= 2:
                            w.append(o_ev[k - 2])
                        b.op("tensor", "transpose", ps_tr_f[k % 2][:],
                             u.x_f[:, dc * TILE + g * 128: dc * TILE + (g + 1) * 128],
                             ident_f[:], waits=w, inc=s_pe)
                        sg = k % 8
                        ew = [(s_pe, b.c(s_pe))]
                        if k >= 8:
                            ew.append(o_st[k - 8])
                        if k % 2 == 0:
                            b.op("vector", "tensor_copy",
                                 out=u.stage[:, sg * 128:(sg + 1) * 128],
                                 in_=ps_tr_f[0][:], waits=ew, inc=s_dve)
                            o_ev[k] = (s_dve, b.c(s_dve))
                        else:
                            b.op("scalar", "copy",
                                 out=u.stage[:, sg * 128:(sg + 1) * 128],
                                 in_=ps_tr_f[1][:], waits=ew, inc=s_act)
                            o_ev[k] = (s_act, b.c(s_act))
                        w = [o_ev[k]]
                        b.op("sync", "dma_start",
                             out=out_d.ap()[t * TILE + g * 128:
                                            t * TILE + (g + 1) * 128,
                                            dc * 128:(dc + 1) * 128],
                             in_=u.stage[:, sg * 128:(sg + 1) * 128],
                             waits=w, inc=STS[4], incn=16)
                        o_st[k] = (STS[4], b.c(STS[4]))

        if debug:
            pairs = [(dbg["x1"], x1_d), (dbg["x2"], x2_d), (dbg["q"], q_d),
                     (dbg["kv"], kvag_out)]
            for dd, ss in pairs:
                b.op("sync", "dma_start", out=dd.ap(), in_=ss.ap(),
                     waits=st_all() + adw(), inc=STS[5], incn=16)

        b.op("sync", None, waits=st_all())

        @block.sync
        def _(e):
            b.realize_engine(e, "sync")

        @block.vector
        def _(e):
            b.realize_engine(e, "vector")

        @block.scalar
        def _(e):
            b.realize_engine(e, "scalar")

        @block.tensor
        def _(e):
            b.realize_engine(e, "tensor")

        @block.gpsimd
        def _(e):
            b.realize_engine(e, "gpsimd")

    return nc


def _host_shards(x):
    rows = [np.concatenate([np.arange(g * 128, (g + 1) * 128) for g in SLOTS[r]])
            for r in range(2)]
    shards, metas = [], []
    for c in range(8):
        bidx, r = c // 2, c % 2
        shards.append(np.ascontiguousarray(x[bidx][rows[r]], dtype=np.float32))
        metas.append(np.array([[128.0 * g for g in SLOTS[r]]], dtype=np.float32))
    return shards, metas, rows


def kernel(L0_attn_x, L0_attn_q_w, L0_attn_kv_w, L0_attn_c_w,
           L0_ffn_w_gate, L0_ffn_w_up, L0_ffn_w_down):
    x = np.asarray(L0_attn_x, dtype=np.float32)
    shards, metas, rows = _host_shards(x)
    weights = {
        "qw": np.ascontiguousarray(L0_attn_q_w, dtype=np.float32),
        "kvw": np.ascontiguousarray(L0_attn_kv_w, dtype=np.float32),
        "cw": np.ascontiguousarray(L0_attn_c_w, dtype=np.float32),
        "wg": np.ascontiguousarray(L0_ffn_w_gate, dtype=np.float32),
        "wu": np.ascontiguousarray(L0_ffn_w_up, dtype=np.float32),
        "wd": np.ascontiguousarray(L0_ffn_w_down, dtype=np.float32),
    }
    nc = build_nc()
    nc.finalize()
    in_maps = [dict(weights, x=shards[c], qpos=metas[c]) for c in range(8)]
    res = run_bass_kernel_spmd(nc, in_maps, core_ids=list(range(8)))
    out = np.empty((4, 4096, 2048), dtype=np.float32)
    for c in range(8):
        out[c // 2][rows[c % 2]] = res.results[c]["out"]
    return out


# version 06 (final)
# speedup vs baseline: 1.6294x; 1.0447x over previous
import numpy as np
from contextlib import ExitStack
import concourse.bass as bass
import concourse.mybir as mybir
from concourse.bass_utils import run_bass_kernel_spmd

F32 = mybir.dt.float32
BF16 = mybir.dt.bfloat16
AF = mybir.ActivationFunctionType
ALU = mybir.AluOpType
AXX = mybir.AxisListType.X

D = 2048
DC = 16
HID = 8192
HC = 64
NT = 2048
T = 4096
TILE = 512
NTILE = 4
HQ = 16
HKV = 4
KB = 512
SCALE = float(1.0 / np.sqrt(128.0))
BSHIFT = -40.0
MASKVAL = -1e30
EPS = 1e-5
RMAGIC = 12582912.0
TWO_PI_HI = 6.28125
TWO_PI_LO = 0.0019353071795864769
LN_THETA = float(np.log(10000.0))
INV2PI = float(1.0 / (2 * np.pi))

SLOTS = [[], []]
for _i in range(8):
    SLOTS[0] += [2 * _i, 31 - 2 * _i]
    SLOTS[1] += [2 * _i + 1, 30 - 2 * _i]
EXT = [g // 4 + 1 for g in SLOTS[0]]
assert EXT == [g // 4 + 1 for g in SLOTS[1]]

ENGS = ("sync", "vector", "scalar", "tensor", "gpsimd")
SWAP32 = [i ^ 1 for i in range(32)]


class Builder:

    def __init__(self):
        self.st = {e: [] for e in ENGS}
        self.cnt = {}

    def op(self, eng, meth, *a, inc=None, incn=1, waits=(), **kw):
        self.st[eng].append((tuple(waits), meth, a, kw, inc, incn))
        if inc is not None:
            nm = inc.name
            self.cnt[nm] = self.cnt.get(nm, 0) + incn
            return self.cnt[nm]
        return None

    def c(self, sem):
        return self.cnt.get(sem.name, 0)

    def realize_engine(self, engobj, eng):
        seen = {}
        for waits, meth, a, kw, inc, incn in self.st[eng]:
            for semh, val in waits:
                if val <= 0 or seen.get(semh.name, -1) >= val:
                    continue
                seen[semh.name] = val
                engobj.wait_ge(semh, val)
            if meth is None:
                continue
            ins = getattr(engobj, meth)(*a, **kw)
            if inc is not None:
                ins.then_inc(inc, incn)


class NS:
    pass


def build_nc(debug=False, cc_groups=None, limit=5, alimit=4):
    nc = bass.Bass("TRN2", target_bir_lowering=False,
                   detect_race_conditions=False)

    x_d = nc.dram_tensor("x", [NT, D], F32, kind="ExternalInput")
    qw_d = nc.dram_tensor("qw", [D, D], F32, kind="ExternalInput")
    kvw_d = nc.dram_tensor("kvw", [1024, D], F32, kind="ExternalInput")
    cw_d = nc.dram_tensor("cw", [D, D], F32, kind="ExternalInput")
    wg_d = nc.dram_tensor("wg", [HID, D], F32, kind="ExternalInput")
    wu_d = nc.dram_tensor("wu", [HID, D], F32, kind="ExternalInput")
    wd_d = nc.dram_tensor("wd", [D, HID], F32, kind="ExternalInput")
    qpos_d = nc.dram_tensor("qpos", [1, 16], F32, kind="ExternalInput")
    out_d = nc.dram_tensor("out", [NT, D], F32, kind="ExternalOutput")

    wg_t = nc.dram_tensor("wg_t", [HID, D], BF16)
    wu_t = nc.dram_tensor("wu_t", [HID, D], BF16)
    wd_t = nc.dram_tensor("wd_t", [D, HID], BF16)
    qw_t = nc.dram_tensor("qw_t", [D, D], BF16)
    kvw_t = nc.dram_tensor("kvw_t", [1024, D], BF16)
    cw_t = nc.dram_tensor("cw_t", [D, D], BF16)
    x1_d = nc.dram_tensor("x1", [D, NT], F32)
    x2_d = nc.dram_tensor("x2", [D, NT], F32)
    q_d = nc.dram_tensor("qbuf", [D, NT], BF16)
    kvag_in = nc.dram_tensor("kvag_in", [1024, NT], BF16)
    kvag_out = nc.dram_tensor("kvag_out", [2048, NT], BF16)
    if debug:
        dbg = {nm: nc.dram_tensor("dbg_" + nm, [D, NT], F32, kind="ExternalOutput")
               for nm in ("x1", "x2")}
        dbg["q"] = nc.dram_tensor("dbg_q", [D, NT], BF16, kind="ExternalOutput")
        dbg["x3"] = nc.dram_tensor("dbg_x3", [D, NT], F32, kind="ExternalOutput")
        dbg["xn5"] = nc.dram_tensor("dbg_xn5", [128, DC * TILE], BF16, kind="ExternalOutput")
        dbg["ropec"] = nc.dram_tensor("dbg_ropec", [128, NTILE * TILE], F32, kind="ExternalOutput")
        dbg["consts"] = nc.dram_tensor("dbg_consts", [128, 10 * 168], F32, kind="ExternalOutput")
        dbg["ropesp"] = nc.dram_tensor("dbg_ropesp", [128, NTILE * TILE], F32, kind="ExternalOutput")
        dbg["kv"] = nc.dram_tensor("dbg_kv", [2048, NT], BF16, kind="ExternalOutput")

    b = Builder()
    es = ExitStack()
    with es:
        sbuf = lambda name, shape, dt: es.enter_context(nc.sbuf_tensor(name, shape, dt))
        sem = lambda name: es.enter_context(nc.semaphore(name))

        s_pe = sem("s_pe")
        s_dve = sem("s_dve")
        s_act = sem("s_act")
        s_pool = sem("s_pool")
        s_w0 = sem("s_w0")
        s_w1 = sem("s_w1")
        s_v0 = sem("s_v0")
        s_v1 = sem("s_v1")
        s_ldt = sem("s_ldt")
        STS = [sem(f"s_st{i}") for i in range(6)]
        s_cc = sem("s_cc")
        WPAR = (s_w0, s_w1)
        VPAR = (s_v0, s_v1)

        def adw():
            return [(s_pe, b.c(s_pe)), (s_dve, b.c(s_dve)),
                    (s_act, b.c(s_act)), (s_pool, b.c(s_pool))]

        def st_all():
            return [(s, b.c(s)) for s in STS]

        ident_f = sbuf("ident_f", [128, 128], F32)
        ident_b = sbuf("ident_b", [128, 128], BF16)
        jtile = sbuf("jtile", [128, 128], F32)
        dtile = sbuf("dtile", [128, KB], F32)
        iota_p = sbuf("iota_p", [128, 1], F32)
        halfp = sbuf("halfp", [128, 1], F32)
        freq = sbuf("freq", [128, 1], F32)
        altsign = sbuf("altsign", [128, 1], F32)
        negB = sbuf("negB", [128, 1], F32)
        ones_row = sbuf("ones_row", [1, 128], F32)
        qpos_sb = sbuf("qpos_sb", [1, 16], F32)
        qpos_cols = sbuf("qpos_cols", [128, 16], F32)
        qf_all = sbuf("qf_all", [128, 16], F32)
        rs4 = sbuf("rs4", [128, 4], F32)
        ssq4 = sbuf("ssq4", [128, 4], F32)
        mean4 = sbuf("mean4", [128, 4], F32)
        esq4 = sbuf("esq4", [128, 4], F32)
        m24 = sbuf("m24", [128, 4], F32)
        var4 = sbuf("var4", [128, 4], F32)
        rvar4 = sbuf("rvar4", [128, 4], F32)
        rstd4 = sbuf("rstd4", [128, 4], F32)
        sums4 = sbuf("sums4", [128, 4], F32)
        recip4 = sbuf("recip4", [128, 4], F32)

        block = es.enter_context(nc.Block())

        def dump_consts(half):
            if not debug:
                return
            col = half * 168
            srcs = [(iota_p, 1), (halfp, 1), (freq, 1), (altsign, 1), (negB, 1),
                    (qf_all, 16), (qpos_cols, 16), (jtile, 128), (dtile, 3)]
            for tsr, wdt in srcs:
                b.op("sync", "dma_start",
                     out=dbg["consts"].ap()[:, col:col + wdt],
                     in_=tsr[:, 0:wdt], waits=adw(), inc=STS[5], incn=16,
                     allow_slow_non_contiguous=True)
                col += wdt

        b.op("gpsimd", "iota", ident_f[:], pattern=[[0, 128]], base=0,
             channel_multiplier=1, allow_small_or_imprecise_dtypes=True)
        b.op("gpsimd", "iota", jtile[:], pattern=[[1, 128]], base=0,
             channel_multiplier=0, allow_small_or_imprecise_dtypes=True)
        b.op("gpsimd", "iota", dtile[:], pattern=[[1, KB]], base=0,
             channel_multiplier=-1, allow_small_or_imprecise_dtypes=True)
        b.op("gpsimd", "iota", iota_p[:], pattern=[[0, 1]], base=0,
             channel_multiplier=1, allow_small_or_imprecise_dtypes=True,
             inc=s_pool)
        g0 = b.c(s_pool)
        b.op("vector", "tensor_tensor", ident_f[:], ident_f[:], jtile[:],
             ALU.is_equal, waits=[(s_pool, g0)], inc=s_dve)
        b.op("vector", "tensor_copy", out=ident_b[:], in_=ident_f[:],
             waits=[(s_dve, b.c(s_dve))], inc=s_dve)
        b.op("vector", "tensor_scalar", out=halfp[:], in0=iota_p[:],
             scalar1=0.5, scalar2=0.25, op0=ALU.mult, op1=ALU.subtract,
             inc=s_dve)
        b.op("vector", "tensor_scalar", out=halfp[:], in0=halfp[:],
             scalar1=RMAGIC, scalar2=RMAGIC, op0=ALU.add, op1=ALU.subtract,
             waits=[(s_dve, b.c(s_dve))], inc=s_dve)
        b.op("vector", "tensor_scalar_mul", freq[:], halfp[:],
             -2.0 * LN_THETA / 128.0, waits=[(s_dve, b.c(s_dve))], inc=s_dve)
        fq = b.c(s_dve)
        b.op("scalar", "activation", out=freq[:], in_=freq[:], func=AF.Exp,
             waits=[(s_dve, fq)], inc=s_act)
        fq_a = b.c(s_act)
        b.op("vector", "tensor_scalar_mul", altsign[:], halfp[:], -2.0,
             inc=s_dve)
        b.op("vector", "tensor_add", altsign[:], altsign[:], iota_p[:],
             waits=[(s_dve, b.c(s_dve))], inc=s_dve)
        b.op("vector", "tensor_scalar", out=altsign[:], in0=altsign[:],
             scalar1=2.0, scalar2=-1.0, op0=ALU.mult, op1=ALU.add,
             waits=[(s_dve, b.c(s_dve))], inc=s_dve)
        b.op("vector", "memset", negB[:], BSHIFT)
        b.op("vector", "memset", ones_row[:], 1.0, inc=s_dve)
        consts_v = b.c(s_dve)

        b.op("sync", "dma_start", out=qpos_sb[:], in_=qpos_d.ap(),
             inc=s_ldt, incn=16)
        qpos_ld = b.c(s_ldt)

        prep_list = [
            (wg_d, wg_t, HC, DC),
            (wu_d, wu_t, HC, DC),
            (wd_d, wd_t, DC, HC),
            (qw_d, qw_t, DC, DC),
            (kvw_d, kvw_t, 8, DC),
            (cw_d, cw_t, DC, DC),
        ]
        with ExitStack() as pes:
            ps_qposb = pes.enter_context(nc.psum_tensor("ps_qposb", [128, 16], F32))
            ps_prep = [pes.enter_context(nc.psum_tensor(f"ps_prep{i}", [128, 128], BF16))
                       for i in range(4)]
            slab_f = pes.enter_context(nc.sbuf_tensor("slab_f", [128, 2 * HID], F32))
            slab_b = pes.enter_context(nc.sbuf_tensor("slab_b", [128, 2 * HID], BF16))
            tout = pes.enter_context(nc.sbuf_tensor("tout", [128, 2 * HID], BF16))

            b.op("tensor", "matmul", ps_qposb[:], ones_row[:], qpos_sb[:],
                 start=True, stop=True,
                 waits=[(s_ldt, qpos_ld), (s_dve, consts_v)], inc=s_pe)
            b.op("vector", "tensor_copy", out=qpos_cols[:], in_=ps_qposb[:],
                 waits=[(s_pe, b.c(s_pe))], inc=s_dve)
            b.op("vector", "tensor_scalar", out=qf_all[:], in0=qpos_cols[:],
                 scalar1=freq[:], scalar2=None, op0=ALU.mult,
                 waits=[(s_act, fq_a), (s_dve, b.c(s_dve))], inc=s_dve)

            slab_i = 0
            tcount = 0
            conv_marks, tr_marks, evac_marks, st_marks = {}, {}, {}, {}
            for src, dst, nrc, ncc in prep_list:
                cols = ncc * 128
                for rc in range(nrc):
                    par = slab_i % 2
                    w = []
                    if slab_i >= 2:
                        w.append((s_dve, conv_marks[slab_i - 2]))
                    b.op("sync", "dma_start",
                         out=slab_f[:, par * HID: par * HID + cols],
                         in_=src.ap()[rc * 128:(rc + 1) * 128, :],
                         waits=w, inc=WPAR[par], incn=16)
                    ldv = b.c(WPAR[par])
                    w = [(WPAR[par], ldv)]
                    if slab_i >= 2:
                        w.append((s_pe, tr_marks[slab_i - 2]))
                    b.op("vector", "tensor_copy",
                         out=slab_b[:, par * HID: par * HID + cols],
                         in_=slab_f[:, par * HID: par * HID + cols],
                         waits=w, inc=s_dve)
                    conv_marks[slab_i] = b.c(s_dve)
                    for cc in range(ncc):
                        pp = tcount % 4
                        w = [(s_dve, conv_marks[slab_i])]
                        if tcount >= 4:
                            w.append(evac_marks[tcount - 4])
                        if cc == 0 and slab_i >= 2:
                            w.append(st_marks[slab_i - 2])
                        b.op("tensor", "transpose", ps_prep[pp][:],
                             slab_b[:, par * HID + cc * 128: par * HID + (cc + 1) * 128],
                             ident_b[:], waits=w, inc=s_pe)
                        pev = b.c(s_pe)
                        w = [(s_pe, pev)]
                        if cc == 0 and slab_i >= 2:
                            w.append(st_marks[slab_i - 2])
                        if tcount % 2 == 0:
                            b.op("vector", "tensor_copy",
                                 out=tout[:, par * HID + cc * 128: par * HID + (cc + 1) * 128],
                                 in_=ps_prep[pp][:], waits=w, inc=s_dve)
                            evac_marks[tcount] = (s_dve, b.c(s_dve))
                        else:
                            b.op("scalar", "copy",
                                 out=tout[:, par * HID + cc * 128: par * HID + (cc + 1) * 128],
                                 in_=ps_prep[pp][:], waits=w, inc=s_act)
                            evac_marks[tcount] = (s_act, b.c(s_act))
                        tcount += 1
                    stsem = STS[slab_i % 2]
                    b.op("sync", "dma_start",
                         out=dst.ap()[rc * 128:(rc + 1) * 128, :],
                         in_=tout[:, par * HID: par * HID + cols],
                         waits=[evac_marks[tcount - 1], evac_marks[tcount - 2]],
                         inc=stsem, incn=16)
                    st_marks[slab_i] = (stsem, b.c(stsem))
                    tr_marks[slab_i] = b.c(s_pe)
                    slab_i += 1
        b.op("sync", None, waits=st_all())
        dump_consts(0)

        pst_hist = []

        def emit_ln(u, from_fm, ps_tr_f, ps_tr_b, xload_marks=None,
                    src_marks=None):
            ln_ev = {}
            if from_fm:
                for g in range(4):
                    for dc in range(DC):
                        k = g * DC + dc
                        w = list(src_marks) if k == 0 else []
                        if k >= 2:
                            w.append(ln_ev[k - 2])
                        b.op("tensor", "transpose", ps_tr_f[k % 2][:],
                             u.x_f[:, dc * TILE + g * 128: dc * TILE + (g + 1) * 128],
                             ident_f[:], waits=w, inc=s_pe)
                        if k % 2 == 0:
                            b.op("vector", "tensor_copy",
                                 out=u.x_tm[:, g * D + dc * 128: g * D + (dc + 1) * 128],
                                 in_=ps_tr_f[0][:], waits=[(s_pe, b.c(s_pe))],
                                 inc=s_dve)
                            ln_ev[k] = (s_dve, b.c(s_dve))
                        else:
                            b.op("scalar", "copy",
                                 out=u.x_tm[:, g * D + dc * 128: g * D + (dc + 1) * 128],
                                 in_=ps_tr_f[1][:], waits=[(s_pe, b.c(s_pe))],
                                 inc=s_act)
                            ln_ev[k] = (s_act, b.c(s_act))
                tm_ready = [(s_dve, b.c(s_dve)), (s_act, b.c(s_act))]
                grp_ready = [tm_ready for _ in range(4)]
            else:
                grp_ready = [[xload_marks[g]] for g in range(4)]

            for g in range(4):
                b.op("scalar", "activation", out=u.sq_scr[:],
                     in_=u.x_tm[:, g * D:(g + 1) * D], func=AF.Square,
                     accum_out=ssq4[:, g:g + 1], waits=grp_ready[g], inc=s_act)
                b.op("vector", "reduce_sum", out=rs4[:, g:g + 1],
                     in_=u.x_tm[:, g * D:(g + 1) * D], axis=AXX,
                     waits=grp_ready[g], inc=s_dve)
            sq_done = b.c(s_act)
            b.op("vector", "tensor_scalar_mul", mean4[:], rs4[:], 1.0 / D,
                 waits=[(s_dve, b.c(s_dve))], inc=s_dve)
            b.op("vector", "tensor_scalar_mul", esq4[:], ssq4[:], 1.0 / D,
                 waits=[(s_act, sq_done)], inc=s_dve)
            b.op("vector", "tensor_mul", m24[:], mean4[:], mean4[:],
                 waits=[(s_dve, b.c(s_dve))], inc=s_dve)
            b.op("vector", "tensor_sub", var4[:], esq4[:], m24[:],
                 waits=[(s_dve, b.c(s_dve))], inc=s_dve)
            b.op("vector", "tensor_scalar_add", var4[:], var4[:], EPS,
                 waits=[(s_dve, b.c(s_dve))], inc=s_dve)
            b.op("vector", "reciprocal", rvar4[:], var4[:],
                 waits=[(s_dve, b.c(s_dve))], inc=s_dve)
            vd = b.c(s_dve)
            b.op("scalar", "activation", out=rstd4[:], in_=rvar4[:],
                 func=AF.Sqrt, waits=[(s_dve, vd)], inc=s_act)
            rstd_done = b.c(s_act)
            nrm_ev = {}
            for g in range(4):
                b.op("vector", "tensor_scalar", out=u.xn_tm[:],
                     in0=u.x_tm[:, g * D:(g + 1) * D],
                     scalar1=mean4[:, g:g + 1], scalar2=rstd4[:, g:g + 1],
                     op0=ALU.subtract, op1=ALU.mult,
                     waits=[(s_act, rstd_done)]
                     + ([nrm_ev[(g - 1) * DC + DC - 1]] if g > 0 else []),
                     inc=s_dve)
                nz = b.c(s_dve)
                for dc in range(DC):
                    k = g * DC + dc
                    w = [(s_dve, nz)]
                    if k >= 2:
                        w.append(nrm_ev[k - 2])
                    b.op("tensor", "transpose", ps_tr_b[k % 2][:],
                         u.xn_tm[:, dc * 128:(dc + 1) * 128], ident_b[:],
                         waits=w, inc=s_pe)
                    if k % 2 == 0:
                        b.op("vector", "tensor_copy",
                             out=u.xn_f[:, dc * TILE + g * 128: dc * TILE + (g + 1) * 128],
                             in_=ps_tr_b[0][:], waits=[(s_pe, b.c(s_pe))],
                             inc=s_dve)
                        nrm_ev[k] = (s_dve, b.c(s_dve))
                    else:
                        b.op("scalar", "copy",
                             out=u.xn_f[:, dc * TILE + g * 128: dc * TILE + (g + 1) * 128],
                             in_=ps_tr_b[1][:], waits=[(s_pe, b.c(s_pe))],
                             inc=s_act)
                        nrm_ev[k] = (s_act, b.c(s_act))
            return [(s_dve, b.c(s_dve)), (s_act, b.c(s_act))]

        def emit_rope_tables(u, t):
            for si in range(4):
                slot = t * 4 + si
                cs = slice(si * 128, (si + 1) * 128)
                b.op("vector", "tensor_scalar", out=u.rsc1[:, cs], in0=jtile[:],
                     scalar1=freq[:], scalar2=qf_all[:, slot:slot + 1],
                     op0=ALU.mult, op1=ALU.add,
                     inc=(s_dve if si == 3 else None))
            b.op("vector", "engine_nop", waits=[(s_dve, b.c(s_dve))])
            for (dst, shift) in ((u.rope_sp, 0.0), (u.rope_c, float(np.pi / 2))):
                if shift:
                    b.op("vector", "tensor_scalar_add", u.rsc1[:], u.rsc1[:],
                         shift, inc=s_dve)
                    b.op("vector", "engine_nop", waits=[(s_dve, b.c(s_dve))])
                b.op("vector", "tensor_scalar", out=u.rsc2[:], in0=u.rsc1[:],
                     scalar1=INV2PI, scalar2=RMAGIC, op0=ALU.mult, op1=ALU.add,
                     inc=s_dve)
                b.op("vector", "tensor_scalar_sub", u.rsc2[:], u.rsc2[:],
                     RMAGIC, waits=[(s_dve, b.c(s_dve))], inc=s_dve)
                b.op("vector", "scalar_tensor_tensor", out=dst[:], in0=u.rsc2[:],
                     scalar=-TWO_PI_HI, in1=u.rsc1[:], op0=ALU.mult,
                     op1=ALU.add, waits=[(s_dve, b.c(s_dve))], inc=s_dve)
                b.op("vector", "scalar_tensor_tensor", out=dst[:], in0=u.rsc2[:],
                     scalar=-TWO_PI_LO, in1=dst[:], op0=ALU.mult, op1=ALU.add,
                     waits=[(s_dve, b.c(s_dve))], inc=s_dve)
                b.op("vector", "tensor_scalar", out=dst[:], in0=dst[:],
                     scalar1=float(-np.pi), scalar2=float(np.pi),
                     op0=ALU.max, op1=ALU.min,
                     waits=[(s_dve, b.c(s_dve))], inc=s_dve)
                sv = b.c(s_dve)
                b.op("scalar", "activation", out=dst[:], in_=dst[:], func=AF.Sin,
                     waits=[(s_dve, sv)], inc=s_act)
            sa = b.c(s_act)
            b.op("vector", "tensor_scalar", out=u.rope_sp[:], in0=u.rope_sp[:],
                 scalar1=altsign[:], scalar2=None, op0=ALU.mult,
                 waits=[(s_act, sa)], inc=s_dve)
            return [(s_act, b.c(s_act)), (s_dve, b.c(s_dve))]

        def emit_proj(u, t, wsrc, nout, dst, rope_pred, ps_pj, xn_marks,
                      rope_marks):
            tc = slice(t * TILE, (t + 1) * TILE)
            pe0 = b.c(s_pe)
            pj_pe, pj_ev = {}, {}
            for oc in range(nout):
                par = oc % 2
                w = [(s_pe, pj_pe[oc - 2])] if oc >= 2 else [(s_pe, pe0)]
                b.op("sync", "dma_start",
                     out=u.wbig[:, par * HID: par * HID + D],
                     in_=wsrc.ap()[oc * 128:(oc + 1) * 128, :],
                     waits=w, inc=WPAR[par], incn=16)
                wl = b.c(WPAR[par])
                w = [(WPAR[par], wl)]
                if oc == 0:
                    w += xn_marks + rope_marks + adw()
                if oc >= 2:
                    w.append(pj_ev[oc - 2])
                for dc in range(DC):
                    b.op("tensor", "matmul", ps_pj[par][:],
                         u.wbig[:, par * HID + dc * 128: par * HID + (dc + 1) * 128],
                         u.xn_f[:, dc * TILE:(dc + 1) * TILE],
                         start=(dc == 0), stop=(dc == DC - 1),
                         waits=(w if dc == 0 else []),
                         inc=(s_pe if dc == DC - 1 else None))
                pj_pe[oc] = b.c(s_pe)
                idx = len(pst_hist)
                sg = idx % 4
                ssl = slice(sg * TILE, (sg + 1) * TILE)
                w = [(s_pe, pj_pe[oc])]
                if idx >= 4:
                    w.append(pst_hist[idx - 4])
                if rope_pred(oc):
                    b.op("vector", "tensor_copy",
                         out=u.silu_sb[:, par * TILE:(par + 1) * TILE],
                         in_=ps_pj[par][:], waits=w, inc=s_dve)
                    b.op("vector", "stream_shuffle", u.rsc1[:],
                         u.silu_sb[:, par * TILE:(par + 1) * TILE], SWAP32)
                    b.op("vector", "tensor_mul", u.rsc2[:],
                         u.silu_sb[:, par * TILE:(par + 1) * TILE], u.rope_c[:])
                    b.op("vector", "tensor_mul", u.rsc1[:], u.rsc1[:],
                         u.rope_sp[:])
                    b.op("vector", "tensor_add", u.pstage[:, ssl], u.rsc2[:],
                         u.rsc1[:], inc=s_dve)
                else:
                    b.op("vector", "tensor_copy", out=u.pstage[:, ssl],
                         in_=ps_pj[par][:], waits=w, inc=s_dve)
                pj_ev[oc] = (s_dve, b.c(s_dve))
                b.op("sync", "dma_start",
                     out=dst.ap()[oc * 128:(oc + 1) * 128, tc],
                     in_=u.pstage[:, ssl], waits=[pj_ev[oc]], inc=STS[2],
                     incn=16)
                pst_hist.append((STS[2], b.c(STS[2])))

        def emit_ffn(u, t, src_kind, src_dram, ps_tr_f, ps_tr_b, ps_ab, ps_dn):
            tc = slice(t * TILE, (t + 1) * TILE)
            if src_kind == "tm":
                xl = []
                for g in range(4):
                    r0 = t * TILE + g * 128
                    w = adw() + st_all() if g == 0 else []
                    b.op("sync", "dma_start", out=u.x_tm[:, g * D:(g + 1) * D],
                         in_=src_dram.ap()[r0:r0 + 128, :],
                         waits=w, inc=s_ldt, incn=16)
                    xl.append((s_ldt, b.c(s_ldt)))
                ln_marks = emit_ln(u, False, ps_tr_f, ps_tr_b, xload_marks=xl)
                xf_ev = {}
                for g in range(4):
                    for dc in range(DC):
                        k = g * DC + dc
                        w = [xl[g]]
                        if k >= 2:
                            w.append(xf_ev[k - 2])
                        b.op("tensor", "transpose", ps_tr_f[k % 2][:],
                             u.x_tm[:, g * D + dc * 128: g * D + (dc + 1) * 128],
                             ident_f[:], waits=w, inc=s_pe)
                        if k % 2 == 0:
                            b.op("vector", "tensor_copy",
                                 out=u.x_f[:, dc * TILE + g * 128: dc * TILE + (g + 1) * 128],
                                 in_=ps_tr_f[0][:], waits=[(s_pe, b.c(s_pe))],
                                 inc=s_dve)
                            xf_ev[k] = (s_dve, b.c(s_dve))
                        else:
                            b.op("scalar", "copy",
                                 out=u.x_f[:, dc * TILE + g * 128: dc * TILE + (g + 1) * 128],
                                 in_=ps_tr_f[1][:], waits=[(s_pe, b.c(s_pe))],
                                 inc=s_act)
                            xf_ev[k] = (s_act, b.c(s_act))
            else:
                for dc in range(DC):
                    w = adw() + st_all() if dc == 0 else []
                    b.op("sync", "dma_start",
                         out=u.x_f[:, dc * TILE:(dc + 1) * TILE],
                         in_=src_dram.ap()[dc * 128:(dc + 1) * 128, tc],
                         waits=w, inc=s_ldt, incn=16)
                xl0 = (s_ldt, b.c(s_ldt))
                ln_marks = emit_ln(u, True, ps_tr_f, ps_tr_b, src_marks=[xl0])

            pe0 = b.c(s_pe)
            f1_pe, f1_sil, f1_mul = {}, {}, {}
            for hc in range(HC):
                par = hc % 2
                w = [(s_pe, f1_pe[hc - 2])] if hc >= 2 else [(s_pe, pe0)]
                b.op("sync", "dma_start",
                     out=u.wbig[:, par * HID: par * HID + D],
                     in_=wg_t.ap()[hc * 128:(hc + 1) * 128, :],
                     waits=w, inc=WPAR[par], incn=16)
                b.op("sync", "dma_start",
                     out=u.wbig[:, par * HID + D: par * HID + 2 * D],
                     in_=wu_t.ap()[hc * 128:(hc + 1) * 128, :],
                     inc=WPAR[par], incn=16)
                wl = b.c(WPAR[par])
                w = [(WPAR[par], wl)]
                if hc == 0:
                    w += ln_marks + adw()
                if hc >= 2:
                    w.append((s_act, f1_sil[hc - 2]))
                    w.append((s_dve, f1_mul[hc - 2]))
                for dc in range(DC):
                    b.op("tensor", "matmul", ps_ab[par][:],
                         u.wbig[:, par * HID + dc * 128: par * HID + (dc + 1) * 128],
                         u.xn_f[:, dc * TILE:(dc + 1) * TILE],
                         start=(dc == 0), stop=(dc == DC - 1),
                         waits=(w if dc == 0 else []),
                         inc=(s_pe if dc == DC - 1 else None))
                geval = b.c(s_pe)
                for dc in range(DC):
                    b.op("tensor", "matmul", ps_ab[2 + par][:],
                         u.wbig[:, par * HID + D + dc * 128: par * HID + D + (dc + 1) * 128],
                         u.xn_f[:, dc * TILE:(dc + 1) * TILE],
                         start=(dc == 0), stop=(dc == DC - 1),
                         inc=(s_pe if dc == DC - 1 else None))
                f1_pe[hc] = b.c(s_pe)
                b.op("scalar", "activation",
                     out=u.silu_sb[:, par * TILE:(par + 1) * TILE],
                     in_=ps_ab[par][:], func=AF.Silu,
                     waits=[(s_pe, geval)], inc=s_act)
                f1_sil[hc] = b.c(s_act)
                b.op("vector", "tensor_mul",
                     u.gated[:, hc * TILE:(hc + 1) * TILE],
                     u.silu_sb[:, par * TILE:(par + 1) * TILE],
                     ps_ab[2 + par][:],
                     waits=[(s_act, f1_sil[hc]), (s_pe, f1_pe[hc])], inc=s_dve)
                f1_mul[hc] = b.c(s_dve)

            pe0 = b.c(s_pe)
            f2_pe, f2_ev = {}, {}
            for oc in range(DC):
                par = oc % 2
                w = [(s_pe, f2_pe[oc - 2])] if oc >= 2 else [(s_pe, pe0)]
                b.op("sync", "dma_start", out=u.wbig[:, par * HID:(par + 1) * HID],
                     in_=wd_t.ap()[oc * 128:(oc + 1) * 128, :],
                     waits=w, inc=WPAR[par], incn=16)
                wl = b.c(WPAR[par])
                w = [(WPAR[par], wl)]
                if oc == 0:
                    w += [(s_dve, b.c(s_dve))] + adw()
                if oc >= 2:
                    w.append(f2_ev[oc - 2])
                for hc in range(HC):
                    b.op("tensor", "matmul", ps_dn[par][:],
                         u.wbig[:, par * HID + hc * 128: par * HID + (hc + 1) * 128],
                         u.gated[:, hc * TILE:(hc + 1) * TILE],
                         start=(hc == 0), stop=(hc == HC - 1),
                         waits=(w if hc == 0 else []),
                         inc=(s_pe if hc == HC - 1 else None))
                f2_pe[oc] = b.c(s_pe)
                b.op("vector", "scalar_tensor_tensor",
                     out=u.x_f[:, oc * TILE:(oc + 1) * TILE],
                     in0=ps_dn[par][:], scalar=0.5,
                     in1=u.x_f[:, oc * TILE:(oc + 1) * TILE],
                     op0=ALU.mult, op1=ALU.add,
                     waits=[(s_pe, f2_pe[oc])], inc=s_dve)
                f2_ev[oc] = (s_dve, b.c(s_dve))
            return f2_ev

        if limit >= 1:
         with ExitStack() as p1:
            u = NS()
            u.x_tm = p1.enter_context(nc.sbuf_tensor("p1_xtm", [128, 4 * D], F32))
            u.xn_tm = p1.enter_context(nc.sbuf_tensor("p1_xntm", [128, D], BF16))
            u.x_f = p1.enter_context(nc.sbuf_tensor("p1_xf", [128, DC * TILE], F32))
            u.xn_f = p1.enter_context(nc.sbuf_tensor("p1_xnf", [128, DC * TILE], BF16))
            u.sq_scr = p1.enter_context(nc.sbuf_tensor("p1_sq", [128, D], BF16))
            u.gated = p1.enter_context(nc.sbuf_tensor("p1_gated", [128, HC * TILE], BF16))
            u.wbig = p1.enter_context(nc.sbuf_tensor("p1_wbig", [128, 2 * HID], BF16))
            u.silu_sb = p1.enter_context(nc.sbuf_tensor("p1_silu", [128, 2 * TILE], F32))
            u.pstage = p1.enter_context(nc.sbuf_tensor("p1_pst", [128, 4 * TILE], BF16))
            u.rope_c = p1.enter_context(nc.sbuf_tensor("p1_rc", [128, TILE], F32))
            u.rope_sp = p1.enter_context(nc.sbuf_tensor("p1_rsp", [128, TILE], F32))
            u.rsc1 = p1.enter_context(nc.sbuf_tensor("p1_rs1", [128, TILE], F32))
            u.rsc2 = p1.enter_context(nc.sbuf_tensor("p1_rs2", [128, TILE], F32))
            ps_tr_f = [p1.enter_context(nc.psum_tensor(f"p1_trf{i}", [128, 128], F32))
                       for i in range(2)]
            ps_tr_b = [p1.enter_context(nc.psum_tensor(f"p1_trb{i}", [128, 128], BF16))
                       for i in range(2)]
            ps_ab = [p1.enter_context(nc.psum_tensor(f"p1_ab{i}", [128, TILE], F32))
                     for i in range(4)]
            ps_dn = [ps_ab[0], ps_ab[1]]

            for t in range(NTILE):
                tc = slice(t * TILE, (t + 1) * TILE)
                res_marks = emit_ffn(u, t, "tm", x_d, ps_tr_f, ps_tr_b,
                                     ps_ab, ps_dn)
                if debug and t == 0:
                    dump_consts(2)
                for dc in range(DC):
                    b.op("sync", "dma_start",
                         out=x1_d.ap()[dc * 128:(dc + 1) * 128, tc],
                         in_=u.x_f[:, dc * TILE:(dc + 1) * TILE],
                         waits=[res_marks[dc]], inc=STS[2], incn=16)
                xn_marks = emit_ln(u, True, ps_tr_f, ps_tr_b,
                                   src_marks=[res_marks[DC - 1],
                                              res_marks[DC - 2]] + adw())
                if debug and t == 0:
                    dump_consts(3)
                rope_marks = emit_rope_tables(u, t)
                if debug:
                    b.op("sync", "dma_start",
                         out=dbg["ropec"].ap()[:, t * TILE:(t + 1) * TILE],
                         in_=u.rope_c[:], waits=rope_marks, inc=STS[5], incn=16)
                    b.op("sync", "dma_start",
                         out=dbg["ropesp"].ap()[:, t * TILE:(t + 1) * TILE],
                         in_=u.rope_sp[:], waits=rope_marks, inc=STS[5], incn=16)
                emit_proj(u, t, kvw_t, 8, kvag_in, lambda oc: oc < 4,
                          [ps_ab[2], ps_ab[3]], xn_marks, rope_marks)
                if debug and t == 0:
                    dump_consts(4)
                emit_proj(u, t, qw_t, DC, q_d, lambda oc: True,
                          [ps_ab[2], ps_ab[3]], xn_marks, rope_marks)
                if debug and t == 0:
                    dump_consts(5)
                if debug and t == 1:
                    dump_consts(6)

        if limit >= 2:
          b.op("gpsimd", "collective_compute", "AllGather", ALU.bypass,
             replica_groups=(cc_groups or [[0, 1], [2, 3], [4, 5], [6, 7]]),
             ins=[kvag_in.ap()], outs=[kvag_out.ap()],
             waits=st_all(), inc=s_cc)

        def owner(g):
            if g <= 15:
                return (g % 2, g if g % 2 == 0 else g - 1)
            return (1 - g % 2, 31 - g if g % 2 == 0 else 32 - g)

        if limit >= 3:
         with ExitStack() as p4:
            a = NS()
            a.kf = p4.enter_context(nc.sbuf_tensor("a_kf", [128, HKV * T], BF16))
            a.vt = p4.enter_context(nc.sbuf_tensor("a_vt", [128, HKV * T], BF16))
            a.qg = p4.enter_context(nc.sbuf_tensor("a_qg", [128, HQ * TILE], BF16))
            a.attn = p4.enter_context(nc.sbuf_tensor("a_attn", [128, HQ * TILE], BF16))
            a.srow = p4.enter_context(nc.sbuf_tensor("a_srow", [128, 2 * T], F32))
            a.p4 = p4.enter_context(nc.sbuf_tensor("a_p4", [128, 4 * T], BF16))
            a.ptsb = p4.enter_context(nc.sbuf_tensor("a_ptsb", [128, 2 * 512], BF16))
            a.masks = p4.enter_context(nc.sbuf_tensor("a_masks", [128, 4 * KB], F32))
            a.resld = p4.enter_context(nc.sbuf_tensor("a_res", [128, 2 * TILE], F32))
            a.x2buf = p4.enter_context(nc.sbuf_tensor("a_x2", [128, 2 * TILE], F32))
            a.cwb = p4.enter_context(nc.sbuf_tensor("a_cwb", [128, 2 * D], BF16))
            a.vstage = p4.enter_context(nc.sbuf_tensor("a_vst", [128, 2 * 128], BF16))

            with ExitStack() as p3:
                ps_vtr = [p3.enter_context(nc.psum_tensor(f"a_vtr{i}", [128, 128], BF16))
                          for i in range(2)]
                v_pe, v_ev = {}, {}
                for h in range(HKV):
                    for g in range(32):
                        r, s = owner(g)
                        b.op("sync", "dma_start",
                             out=a.kf[:, h * T + g * 128: h * T + (g + 1) * 128],
                             in_=kvag_out.ap()[r * 1024 + h * 128:
                                               r * 1024 + (h + 1) * 128,
                                               s * 128:(s + 1) * 128],
                             waits=[(s_cc, 1)], inc=s_ldt, incn=16)
                        k = h * 32 + g
                        w = [(s_cc, 1)]
                        if k >= 2:
                            w.append((s_pe, v_pe[k - 2]))
                        b.op("sync", "dma_start",
                             out=a.vstage[:, (k % 2) * 128:(k % 2 + 1) * 128],
                             in_=kvag_out.ap()[r * 1024 + 512 + h * 128:
                                               r * 1024 + 512 + (h + 1) * 128,
                                               s * 128:(s + 1) * 128],
                             waits=w, inc=VPAR[k % 2], incn=16)
                        w = [(VPAR[k % 2], b.c(VPAR[k % 2]))]
                        if k == 0:
                            w += adw()
                        if k >= 2:
                            w.append(v_ev[k - 2])
                        b.op("tensor", "transpose", ps_vtr[k % 2][:],
                             a.vstage[:, (k % 2) * 128:(k % 2 + 1) * 128],
                             ident_b[:], waits=w, inc=s_pe)
                        v_pe[k] = b.c(s_pe)
                        if k % 2 == 0:
                            b.op("vector", "tensor_copy",
                                 out=a.vt[:, k * 128:(k + 1) * 128],
                                 in_=ps_vtr[0][:], waits=[(s_pe, v_pe[k])],
                                 inc=s_dve)
                            v_ev[k] = (s_dve, b.c(s_dve))
                        else:
                            b.op("scalar", "copy",
                                 out=a.vt[:, k * 128:(k + 1) * 128],
                                 in_=ps_vtr[1][:], waits=[(s_pe, v_pe[k])],
                                 inc=s_act)
                            v_ev[k] = (s_act, b.c(s_act))
                kv_ready = [(s_ldt, b.c(s_ldt)), (s_dve, b.c(s_dve)),
                            (s_act, b.c(s_act))]

            for t in (range(NTILE) if limit >= 4 else []):
                tc = slice(t * TILE, (t + 1) * TILE)
                for hq in range(HQ):
                    w = kv_ready + [(s_pe, b.c(s_pe))] if hq == 0 else []
                    b.op("sync", "dma_start",
                         out=a.qg[:, hq * TILE:(hq + 1) * TILE],
                         in_=q_d.ap()[hq * 128:(hq + 1) * 128, tc],
                         waits=w, inc=s_ldt, incn=16)
                q_ready = (s_ldt, b.c(s_ldt))
                for si in range(4):
                    slot = t * 4 + si
                    ms = slice(si * KB, (si + 1) * KB)
                    b.op("vector", "tensor_scalar", out=a.masks[:, ms],
                         in0=dtile[:], scalar1=qpos_cols[:, slot:slot + 1],
                         scalar2=float(512.0 * (EXT[slot] - 1)),
                         op0=ALU.subtract, op1=ALU.add,
                         waits=(adw() if si == 0 else []))
                    b.op("vector", "tensor_single_scalar", out=a.masks[:, ms],
                         in_=a.masks[:, ms], scalar=0.0, op=ALU.is_gt)
                    b.op("vector", "tensor_scalar_mul", a.masks[:, ms],
                         a.masks[:, ms], MASKVAL, inc=s_dve)
                mask_ready = (s_dve, b.c(s_dve))

                with ExitStack() as pi:
                    if alimit < 1:
                        continue
                    ps_s = [pi.enter_context(nc.psum_tensor(f"a_s{t}{i}", [128, KB], F32))
                            for i in range(2)]
                    ps_pt = [pi.enter_context(nc.psum_tensor(f"a_pt{t}{i}", [128, 128], BF16))
                             for i in range(4)]
                    ps_pv = pi.enter_context(nc.psum_tensor(f"a_pv{t}", [128, 512], F32))
                    sbank = [None, None]
                    srow_free = [0, 0]
                    ptlast = None
                    pvlast = None
                    first = True
                    for si in range(4):
                        slot = t * 4 + si
                        e = EXT[slot]
                        nch = 4 * e
                        for kvh in range(HKV):
                            for qh in range(4):
                                hq = kvh * 4 + qh
                                so = (qh % 2) * T
                                for blk in range(e):
                                    bk = blk % 2
                                    w = []
                                    if first:
                                        w += [q_ready, mask_ready] + adw()
                                        first = False
                                    if sbank[bk] is not None:
                                        w.append(sbank[bk])
                                    b.op("tensor", "matmul", ps_s[bk][:],
                                         a.qg[:, hq * TILE + si * 128:
                                              hq * TILE + (si + 1) * 128],
                                         a.kf[:, kvh * T + blk * KB:
                                              kvh * T + (blk + 1) * KB],
                                         start=True, stop=True, waits=w,
                                         inc=s_pe)
                                    pm = b.c(s_pe)
                                    if blk == e - 1:
                                        b.op("vector", "tensor_tensor",
                                             out=a.srow[:, so + blk * KB:
                                                        so + (blk + 1) * KB],
                                             in0=ps_s[bk][:],
                                             in1=a.masks[:, si * KB:(si + 1) * KB],
                                             op=ALU.add,
                                             waits=[(s_pe, pm),
                                                    (s_act, srow_free[qh % 2])],
                                             inc=s_dve)
                                        sbank[bk] = (s_dve, b.c(s_dve))
                                    else:
                                        b.op("scalar", "copy",
                                             out=a.srow[:, so + blk * KB:
                                                        so + (blk + 1) * KB],
                                             in_=ps_s[bk][:],
                                             waits=[(s_pe, pm),
                                                    (s_act, srow_free[qh % 2])],
                                             inc=s_act)
                                        sbank[bk] = (s_act, b.c(s_act))
                                evw = [sbank[0], sbank[1],
                                       (s_pe, b.c(s_pe))]
                                evw = [x for x in evw if x is not None]
                                b.op("scalar", "activation",
                                     out=a.p4[:, qh * T: qh * T + e * KB],
                                     in_=a.srow[:, so:so + e * KB], func=AF.Exp,
                                     bias=negB[:], scale=SCALE,
                                     accum_out=sums4[:, qh:qh + 1],
                                     waits=evw, inc=s_act)
                                srow_free[qh % 2] = b.c(s_act)
                            exp_done = b.c(s_act)
                            if alimit < 2:
                                continue
                            b.op("vector", "reciprocal", recip4[:], sums4[:],
                                 waits=[(s_act, exp_done)], inc=s_dve)
                            rc_done = b.c(s_dve)
                            for qh in range(4):
                                b.op("vector", "tensor_scalar",
                                     out=a.p4[:, qh * T: qh * T + e * KB],
                                     in0=a.p4[:, qh * T: qh * T + e * KB],
                                     scalar1=recip4[:, qh:qh + 1], scalar2=None,
                                     op0=ALU.mult, waits=[(s_dve, rc_done)],
                                     inc=s_dve)
                            pnorm = b.c(s_dve)
                            if alimit < 3:
                                continue
                            for c in range(nch):
                                w = [(s_dve, pnorm)] if c == 0 else []
                                if c == 0 and pvlast is not None:
                                    w += pvlast
                                if ptlast is not None:
                                    w += ptlast
                                for qh in range(4):
                                    b.op("tensor", "transpose", ps_pt[qh][:],
                                         a.p4[:, qh * T + c * 128:
                                              qh * T + (c + 1) * 128],
                                         ident_b[:],
                                         waits=(w if qh == 0 else []),
                                         inc=(s_pe if qh == 3 else None))
                                trm = b.c(s_pe)
                                cs = (c % 2) * 512
                                for qh in range(4):
                                    if qh % 2 == 0:
                                        b.op("vector", "tensor_copy",
                                             out=a.ptsb[:, cs + qh * 128:
                                                        cs + (qh + 1) * 128],
                                             in_=ps_pt[qh][:],
                                             waits=[(s_pe, trm)], inc=s_dve)
                                    else:
                                        b.op("scalar", "copy",
                                             out=a.ptsb[:, cs + qh * 128:
                                                        cs + (qh + 1) * 128],
                                             in_=ps_pt[qh][:],
                                             waits=[(s_pe, trm)], inc=s_act)
                                ptlast = [(s_dve, b.c(s_dve)),
                                          (s_act, b.c(s_act))]
                                b.op("tensor", "matmul", ps_pv[:],
                                     a.vt[:, (kvh * 32 + c) * 128:
                                          (kvh * 32 + c + 1) * 128],
                                     a.ptsb[:, cs:cs + 512],
                                     start=(c == 0), stop=(c == nch - 1),
                                     waits=ptlast,
                                     inc=(s_pe if c == nch - 1 else None))
                            pv_done = b.c(s_pe)
                            oc0 = (4 * kvh) * TILE + si * 128
                            b.op("vector", "tensor_copy",
                                 out=a.attn[:, :].rearrange(
                                     "p (h u) -> p h u", h=HQ)[
                                     :, 4 * kvh:4 * kvh + 4,
                                     si * 128:(si + 1) * 128],
                                 in_=ps_pv[:, :].rearrange(
                                     "p (h u) -> p h u", h=4),
                                 waits=[(s_pe, pv_done)], inc=s_dve)
                            pvlast = [(s_dve, b.c(s_dve))]
                    attn_all = [(s_dve, b.c(s_dve)), (s_act, b.c(s_act))]

                if alimit < 4:
                    continue
                with ExitStack() as pc:
                    ps_c = [pc.enter_context(nc.psum_tensor(f"a_c{t}{i}", [128, TILE], F32))
                            for i in range(2)]
                    pe0 = b.c(s_pe)
                    cp_pe, cp_ev, cp_st = {}, {}, {}
                    for oc in range(DC):
                        par = oc % 2
                        w = [(s_pe, cp_pe[oc - 2])] if oc >= 2 else [(s_pe, pe0)]
                        b.op("sync", "dma_start",
                             out=a.cwb[:, par * D:(par + 1) * D],
                             in_=cw_t.ap()[oc * 128:(oc + 1) * 128, :],
                             waits=w, inc=WPAR[par], incn=16)
                        wl = b.c(WPAR[par])
                        w2 = [cp_ev[oc - 2]] if oc >= 2 else []
                        b.op("sync", "dma_start",
                             out=a.resld[:, par * TILE:(par + 1) * TILE],
                             in_=x1_d.ap()[oc * 128:(oc + 1) * 128, tc],
                             waits=w2, inc=VPAR[par], incn=16)
                        rl = b.c(VPAR[par])
                        w = [(WPAR[par], wl)]
                        if oc == 0:
                            w += attn_all + adw()
                        if oc >= 2:
                            w.append(cp_ev[oc - 2])
                        for dc in range(DC):
                            b.op("tensor", "matmul", ps_c[par][:],
                                 a.cwb[:, par * D + dc * 128: par * D + (dc + 1) * 128],
                                 a.attn[:, dc * TILE:(dc + 1) * TILE],
                                 start=(dc == 0), stop=(dc == DC - 1),
                                 waits=(w if dc == 0 else []),
                                 inc=(s_pe if dc == DC - 1 else None))
                        cp_pe[oc] = b.c(s_pe)
                        w = [(s_pe, cp_pe[oc]), (VPAR[par], rl)]
                        if oc >= 2:
                            w.append(cp_st[oc - 2])
                        b.op("vector", "tensor_add",
                             a.x2buf[:, par * TILE:(par + 1) * TILE],
                             ps_c[par][:], a.resld[:, par * TILE:(par + 1) * TILE],
                             waits=w, inc=s_dve)
                        cp_ev[oc] = (s_dve, b.c(s_dve))
                        b.op("sync", "dma_start",
                             out=x2_d.ap()[oc * 128:(oc + 1) * 128, tc],
                             in_=a.x2buf[:, par * TILE:(par + 1) * TILE],
                             waits=[cp_ev[oc]], inc=STS[3], incn=16)
                        cp_st[oc] = (STS[3], b.c(STS[3]))

        if limit >= 5:
         with ExitStack() as p5:
            u = NS()
            u.x_tm = p5.enter_context(nc.sbuf_tensor("p5_xtm", [128, 4 * D], F32))
            u.xn_tm = p5.enter_context(nc.sbuf_tensor("p5_xntm", [128, D], BF16))
            u.x_f = p5.enter_context(nc.sbuf_tensor("p5_xf", [128, DC * TILE], F32))
            u.xn_f = p5.enter_context(nc.sbuf_tensor("p5_xnf", [128, DC * TILE], BF16))
            u.sq_scr = p5.enter_context(nc.sbuf_tensor("p5_sq", [128, D], BF16))
            u.gated = p5.enter_context(nc.sbuf_tensor("p5_gated", [128, HC * TILE], BF16))
            u.wbig = p5.enter_context(nc.sbuf_tensor("p5_wbig", [128, 2 * HID], BF16))
            u.silu_sb = p5.enter_context(nc.sbuf_tensor("p5_silu", [128, 2 * TILE], F32))
            u.stage = p5.enter_context(nc.sbuf_tensor("p5_stage", [128, 8 * 128], F32))
            ps_tr_f = [p5.enter_context(nc.psum_tensor(f"p5_trf{i}", [128, 128], F32))
                       for i in range(2)]
            ps_tr_b = [p5.enter_context(nc.psum_tensor(f"p5_trb{i}", [128, 128], BF16))
                       for i in range(2)]
            ps_ab = [p5.enter_context(nc.psum_tensor(f"p5_ab{i}", [128, TILE], F32))
                     for i in range(4)]
            ps_dn = [ps_ab[0], ps_ab[1]]

            for t in range(NTILE):
                res_marks = emit_ffn(u, t, "fm", x2_d, ps_tr_f, ps_tr_b,
                                     ps_ab, ps_dn)
                if debug:
                    if t == 0:
                        b.op("sync", "dma_start", out=dbg["xn5"].ap(),
                             in_=u.xn_f[:], waits=[(s_dve, b.c(s_dve)),
                                                   (s_act, b.c(s_act))],
                             inc=STS[5], incn=16)
                    for dc in range(DC):
                        b.op("sync", "dma_start",
                             out=dbg["x3"].ap()[dc * 128:(dc + 1) * 128,
                                                t * TILE:(t + 1) * TILE],
                             in_=u.x_f[:, dc * TILE:(dc + 1) * TILE],
                             waits=[res_marks[dc]], inc=STS[5], incn=16)
                o_ev, o_st = {}, {}
                for dc in range(DC):
                    for g in range(4):
                        k = dc * 4 + g
                        w = [res_marks[dc]]
                        if k >= 2:
                            w.append(o_ev[k - 2])
                        b.op("tensor", "transpose", ps_tr_f[k % 2][:],
                             u.x_f[:, dc * TILE + g * 128: dc * TILE + (g + 1) * 128],
                             ident_f[:], waits=w, inc=s_pe)
                        sg = k % 8
                        ew = [(s_pe, b.c(s_pe))]
                        if k >= 8:
                            ew.append(o_st[k - 8])
                        if k % 2 == 0:
                            b.op("vector", "tensor_copy",
                                 out=u.stage[:, sg * 128:(sg + 1) * 128],
                                 in_=ps_tr_f[0][:], waits=ew, inc=s_dve)
                            o_ev[k] = (s_dve, b.c(s_dve))
                        else:
                            b.op("scalar", "copy",
                                 out=u.stage[:, sg * 128:(sg + 1) * 128],
                                 in_=ps_tr_f[1][:], waits=ew, inc=s_act)
                            o_ev[k] = (s_act, b.c(s_act))
                        w = [o_ev[k]]
                        b.op("sync", "dma_start",
                             out=out_d.ap()[t * TILE + g * 128:
                                            t * TILE + (g + 1) * 128,
                                            dc * 128:(dc + 1) * 128],
                             in_=u.stage[:, sg * 128:(sg + 1) * 128],
                             waits=w, inc=STS[4], incn=16)
                        o_st[k] = (STS[4], b.c(STS[4]))

        if debug:
            pairs = [(dbg["x1"], x1_d), (dbg["x2"], x2_d), (dbg["q"], q_d),
                     (dbg["kv"], kvag_out)]
            for dd, ss in pairs:
                b.op("sync", "dma_start", out=dd.ap(), in_=ss.ap(),
                     waits=st_all() + adw(), inc=STS[5], incn=16)

        b.op("sync", None, waits=st_all())

        @block.sync
        def _(e):
            b.realize_engine(e, "sync")

        @block.vector
        def _(e):
            b.realize_engine(e, "vector")

        @block.scalar
        def _(e):
            b.realize_engine(e, "scalar")

        @block.tensor
        def _(e):
            b.realize_engine(e, "tensor")

        @block.gpsimd
        def _(e):
            b.realize_engine(e, "gpsimd")

    return nc


def _host_shards(x):
    rows = [np.concatenate([np.arange(g * 128, (g + 1) * 128) for g in SLOTS[r]])
            for r in range(2)]
    shards, metas = [], []
    for c in range(8):
        bidx, r = c // 2, c % 2
        shards.append(np.ascontiguousarray(x[bidx][rows[r]], dtype=np.float32))
        metas.append(np.array([[128.0 * g for g in SLOTS[r]]], dtype=np.float32))
    return shards, metas, rows


def kernel(L0_attn_x, L0_attn_q_w, L0_attn_kv_w, L0_attn_c_w,
           L0_ffn_w_gate, L0_ffn_w_up, L0_ffn_w_down):
    x = np.asarray(L0_attn_x, dtype=np.float32)
    shards, metas, rows = _host_shards(x)
    weights = {
        "qw": np.ascontiguousarray(L0_attn_q_w, dtype=np.float32),
        "kvw": np.ascontiguousarray(L0_attn_kv_w, dtype=np.float32),
        "cw": np.ascontiguousarray(L0_attn_c_w, dtype=np.float32),
        "wg": np.ascontiguousarray(L0_ffn_w_gate, dtype=np.float32),
        "wu": np.ascontiguousarray(L0_ffn_w_up, dtype=np.float32),
        "wd": np.ascontiguousarray(L0_ffn_w_down, dtype=np.float32),
    }
    nc = build_nc()
    nc.finalize()
    in_maps = [dict(weights, x=shards[c], qpos=metas[c]) for c in range(8)]
    res = run_bass_kernel_spmd(nc, in_maps, core_ids=list(range(8)))
    out = np.empty((4, 4096, 2048), dtype=np.float32)
    for c in range(8):
        out[c // 2][rows[c % 2]] = res.results[c]["out"]
    return out
